# version 13
# baseline (speedup 1.0000x reference)
import contextlib
import numpy as np
import ml_dtypes
import concourse.bass as bass
import concourse.mybir as mybir
from concourse.bass_utils import run_bass_kernel_spmd

F32 = mybir.dt.float32
BF16 = mybir.dt.bfloat16
I32 = mybir.dt.int32
ALU = mybir.AluOpType
ACT = mybir.ActivationFunctionType
AX = mybir.AxisListType.X

D = 1024
NT = 16
NH = 16
NC = 8
EPS = 1e-5
SCALE = float(64) ** -0.5
MAGIC = float(0x5F3759DF)
FUSED_T0 = 3


class Ledger:

    STREAMS = ("T", "V", "S", "G", "SY")

    def __init__(self):
        self.recs = {e: [] for e in self.STREAMS}
        self.sem_owner = {}

    def emit(self, eng, deps, fn, tag="", inc_sem=None, inc_amt=None, war=()):
        lst = self.recs[eng]
        deps = [d for d in deps if d is not None]
        war = [d for d in war if d is not None and d[0] != eng]
        for (de, dv) in list(deps) + war:
            assert dv <= len(self.recs[de]), f"{eng}:{tag} dep on future {de}:{dv}"
            r = self.recs[de][dv - 1]
            if de != eng and r["inc_sem"] is None:
                r["inc_sem"] = de
                r["inc_amt"] = 1
        if inc_sem is not None and inc_sem not in self.STREAMS:
            own = self.sem_owner.setdefault(inc_sem, eng)
            assert own == eng, f"sem {inc_sem} used from {own} and {eng}"
        rec = {"deps": deps, "war": war, "fn": fn, "tag": tag,
               "inc_sem": inc_sem, "inc_amt": inc_amt}
        lst.append(rec)
        return (eng, len(lst))

    def emit_dma(self, eng, deps, fn, tag="", sem=None):
        return self.emit(eng, deps, fn, tag=tag, inc_sem=sem or eng, inc_amt=16)

    def build_runner(self, eng, sems):
        items = self.recs[eng]
        cum = {}
        semval = []
        for rec in items:
            s = rec["inc_sem"]
            if s is not None:
                cum[s] = cum.get(s, 0) + rec["inc_amt"]
            semval.append(dict(cum))

        def runner(h):
            last_wait = {}
            drained = 0
            for i, rec in enumerate(items):
                need_drain = False
                for (de, dv) in rec["deps"] + rec.get("war", []):
                    if de == eng:
                        drec = self.recs[de][dv - 1]
                        if drec["inc_amt"] == 16 and drec["inc_sem"] is not None:
                            sname = drec["inc_sem"]
                            v = 0
                            for j in range(dv):
                                r2 = self.recs[de][j]
                                if r2["inc_sem"] == sname:
                                    v += r2["inc_amt"]
                            if v > last_wait.get(sname, -1):
                                h.wait_ge(sems[sname], v)
                                last_wait[sname] = v
                        elif dv > drained:
                            need_drain = True
                        continue
                    drec = self.recs[de][dv - 1]
                    sname = drec["inc_sem"]
                    v = 0
                    for j in range(dv):
                        r2 = self.recs[de][j]
                        if r2["inc_sem"] == sname:
                            v += r2["inc_amt"]
                    if v > last_wait.get(sname, -1):
                        h.wait_ge(sems[sname], v)
                        last_wait[sname] = v
                if need_drain:
                    h.drain()
                    drained = i
                last = rec["fn"](h)
                if rec["inc_sem"] is not None:
                    assert last is not None, f"{eng} '{rec['tag']}' no instr to inc"
                    last.then_inc(sems[rec["inc_sem"]], rec["inc_amt"])
        return runner


def build_program(debug=False, ntiles=NT, stop_stage=99):
    nc = bass.Bass("TRN2", target_bir_lowering=False)

    def din(name, shape, dt=F32):
        return nc.dram_tensor(name, list(shape), dt, kind="ExternalInput").ap()

    x_d = din("x", [2048, D])
    wnames = ("wq", "wk", "wv", "wo", "w1", "wa1", "wa2")
    w_d = {n: din("w_" + n, [D, D]) for n in wnames}
    bq_d = din("bq", [D]); bk_d = din("bk", [D])
    bv_d = din("bv", [D]); bo_d = din("bo", [D]); bf_d = din("bf", [D])
    dec_d = din("dec", [1, NH])
    i128_d = din("i128", [128, 128], BF16)
    ones_d = din("ones", [1, 128], BF16)
    onesf_d = din("onesf", [1, 128])
    pos_d = din("pos", [1, 128])
    par_d = din("par", [128, 1])
    out_d = nc.dram_tensor("out", [2048, D], F32, kind="ExternalOutput").ap()
    t1t_hbm = nc.dram_tensor("t1t_scratch", [128, NC * D], F32, kind="Internal").ap()

    dbg = {}
    if debug:
        for n, shape, dt in [
                ("ln1b", [128, D], BF16), ("lnfm", [128, D], BF16),
                ("qp", [128, D], BF16), ("kz", [128, 2048], BF16),
                ("vz", [128, 2048], BF16), ("esb", [128, 2064], BF16),
                ("negmx", [128, 16], F32), ("zsum", [128, 16], F32),
                ("atsb", [128, 2048], BF16), ("aofm", [128, D], BF16),
                ("nrm2", [128, D], F32), ("x2", [128, D], F32),
                ("t3", [128, D], F32), ("dfac", [128, 16], F32),
                ("m2", [128, D], BF16), ("aofm1", [128, D], BF16),
                ("x2b", [128, D], F32), ("rstd1", [128, 1], F32),
                ("vcp0", [2, 1024], BF16), ("atcp1", [2, 1024], BF16),
                ("h1", [128, D], BF16)]:
            dbg[n] = nc.dram_tensor("dbg_" + n, shape, dt, kind="ExternalOutput").ap()

    st = contextlib.ExitStack()
    with st:
        def sbuf(name, shape, dt=F32):
            return st.enter_context(nc.sbuf_tensor(name, shape, dt))

        def psum(name, shape):
            return st.enter_context(nc.psum_tensor(name, shape, F32))

        wsb = {n: sbuf("sb_" + n, [128, NC * D], BF16) for n in wnames}
        wstage = [sbuf("wst0", [128, 512]), sbuf("wst1", [128, 512])]
        bqf = sbuf("bqf", [128, NC]); bkf = sbuf("bkf", [128, NC])
        brow = sbuf("brow", [1, D])
        bvr = sbuf("bvr", [1, D], BF16); bor = sbuf("bor", [1, D], BF16)
        bfr = sbuf("bfr", [1, D], BF16)
        dec_sb = sbuf("dec_sb", [1, NH]); dsc1 = sbuf("dsc1", [1, NH])
        dsc2 = sbuf("dsc2", [1, NH])
        dfac = sbuf("dfac", [128, NH]); dtmp = sbuf("dtmp", [128, NH])
        dfbf = sbuf("dfbf", [128, NH], BF16)
        m2 = sbuf("m2", [128, D], BF16)
        m2row = sbuf("m2row", [1, 128], BF16)
        i128 = sbuf("i128_sb", [128, 128], BF16)
        ones_bf = sbuf("ones_bf", [1, 128], BF16)
        onesf = sbuf("onesf_sb", [1, 128])
        pos_sb = sbuf("pos_sb", [1, 128])
        par_sb = sbuf("par_sb", [128, 1])

        xs = [sbuf(f"x{i}", [128, D]) for i in range(4)]
        lnb = [sbuf(f"lnb{i}", [128, D], BF16) for i in range(2)]
        lnb3 = sbuf("lnb3", [128, D], BF16)
        lnfm0 = [sbuf(f"lnfm0{i}", [128, D], BF16) for i in range(2)]
        lnfm1 = sbuf("lnfm1", [128, D], BF16)
        qp = [sbuf("qp0", [128, D], BF16)]
        qp.append(qp[0])
        kz = [sbuf(f"kz{i}", [128, 2048], BF16) for i in range(2)]
        vz = [sbuf(f"vz{i}", [128, 2048], BF16) for i in range(2)]
        vcp = [sbuf(f"vcp{i}", [2, 1024], BF16) for i in range(3)]
        esb = [sbuf(f"esb{i}", [128, 2064], BF16) for i in range(2)]
        negmx = sbuf("negmx", [128, NH])
        xcar = sbuf("xcar", [128, NH])
        zsum = sbuf("zsum", [128, NH])
        ecar = sbuf("ecar", [128, NH])
        rz = sbuf("rz", [128, NH])
        atsb = sbuf("atsb", [128, 1024], BF16)
        atc16 = sbuf("atc16", [16, 128], BF16)
        atcp = sbuf("atcp", [2, 1024], BF16)
        aofm = sbuf("aofm", [128, D], BF16)
        nrm = [sbuf("nrm0", [128, D])]
        nrm.append(nrm[0])
        t3sb = nrm[0]
        h1fm = sbuf("h1fm", [128, D], BF16)
        h2fm = sbuf("h2fm", [128, D], BF16)
        gslots = [h1fm[:, :].bitcast(F32), h2fm[:, :].bitcast(F32)]
        bn12 = sbuf("bn12", [128, 24])
        smt = {k: [sbuf(f"sm_{k}{i}", [128, 8]) for i in range(2)]
               for k in (1, 2, 3, 4)}
        smi = {k: [sbuf(f"smi_{k}{i}", [128, 1], I32) for i in range(2)]
               for k in (1, 2, 3, 4)}

        ps_a = psum("ps_a", [128, 1024])
        ps_b = psum("ps_b", [128, 1024])
        ps_c = psum("ps_c", [128, 1024])
        ps_d = psum("ps_d", [128, 1024])

        sems = {e: st.enter_context(nc.semaphore("sem_" + e))
                for e in Ledger.STREAMS}
        for extra in ("cst", "brow", "wst0", "wst1", "wst2", "wst3",
                      "x0", "x1", "x2", "x3", "xp0", "xp1",
                      "vc0", "vc1", "vc2", "atcp", "do0", "do1", "do2", "do3"):
            sems[extra] = st.enter_context(nc.semaphore("sem_" + extra))
        block = st.enter_context(nc.Block())

        L = Ledger()
        tok = {}

        slot_last = [None, None, None, None]
        slot_ctr = [0]
        cast_tok = {}

        def load_weight_chunk(name, c, eng, q="SY"):
            qbase = 0 if q == "SY" else 2
            ctoks = []
            for pi in range(2):
                slot = qbase + pi
                dst = (wstage[slot][:, :] if slot < 2 else gslots[slot - 2])
                deps = [slot_last[slot]] if slot_last[slot] else []
                dtok = L.emit_dma(q, deps, lambda h, name=name, c=c, pi=pi,
                                  dst=dst: h.dma_start(
                    out=dst,
                    in_=w_d[name][c * 128:(c + 1) * 128,
                                  pi * 512:(pi + 1) * 512]),
                    tag=f"dma_{name}{c}_{pi}", sem=f"wst{slot}")
                o0 = c * D + pi * 512
                if eng == "S":
                    def castfn(h, o0=o0, name=name, dst=dst):
                        return h.copy(out=wsb[name][:, o0:o0 + 512], in_=dst)
                else:
                    def castfn(h, o0=o0, name=name, dst=dst):
                        return h.tensor_copy(out=wsb[name][:, o0:o0 + 512],
                                             in_=dst)
                ctok = L.emit(eng, [dtok], castfn, tag=f"cast_{name}{c}_{pi}")
                slot_last[slot] = ctok
                ctoks.append(ctok)
            cast_tok[(name, c)] = ctoks[-1]

        def load_weight(name, eng, q="SY", chunks=None):
            for c in (chunks if chunks is not None else range(NC)):
                load_weight_chunk(name, c, eng, q)

        for t in (0, 1):
            tok[("xin", t)] = L.emit_dma(
                "SY", [], lambda h, t=t: h.dma_start(
                    out=xs[t][:, :], in_=x_d[t * 128:(t + 1) * 128, :]),
                tag=f"xin{t}", sem=f"xp{t}")

        L.emit_dma("SY", [], lambda h: h.dma_start(
            out=bqf[:, :], in_=bq_d.rearrange("(c p) -> p c", p=128),
            allow_slow_non_contiguous=True), tag="bq", sem="cst")
        L.emit_dma("SY", [], lambda h: h.dma_start(
            out=bkf[:, :], in_=bk_d.rearrange("(c p) -> p c", p=128),
            allow_slow_non_contiguous=True), tag="bk", sem="cst")
        last_cst = None
        for nm, dst, src in (("dec", dec_sb, dec_d), ("i128", i128, i128_d),
                             ("ones", ones_bf, ones_d), ("onesf", onesf, onesf_d),
                             ("pos", pos_sb, pos_d), ("par", par_sb, par_d)):
            last_cst = L.emit_dma("SY", [], lambda h, dst=dst, src=src:
                                  h.dma_start(out=dst[:, :], in_=src[:, :]),
                                  tag=nm, sem="cst")
        for nm in ("bq", "bk", "dec", "i128", "ones", "onesf", "pos", "par"):
            tok[nm] = last_cst

        prev = None
        for nm, dst, src in (("bv", bvr, bv_d), ("bo", bor, bo_d), ("bf", bfr, bf_d)):
            dtok = L.emit_dma("SY", [prev] if prev else [], lambda h, src=src:
                              h.dma_start(out=brow[0:1, :], in_=src[None, :]),
                              tag="row_" + nm, sem="brow")
            prev = L.emit("S", [dtok], lambda h, dst=dst: h.activation(
                out=dst[0:1, :], in_=brow[0:1, :], func=ACT.Copy), tag="cast_" + nm)
            tok["row_" + nm] = prev

        tok["zkz0"] = L.emit("V", [], lambda h: h.memset(kz[0][:, :], 0.0), tag="zkz0")
        tok["zkz1"] = L.emit("V", [], lambda h: h.memset(kz[1][:, :], 0.0), tag="zkz1")
        tok["zvz0"] = L.emit("V", [], lambda h: h.memset(vz[0][:, :], 0.0), tag="zvz0")
        tok["zvz1"] = L.emit("V", [], lambda h: h.memset(vz[1][:, :], 0.0), tag="zvz1")
        tok["zvc1"] = L.emit("V", [], lambda h: h.memset(vcp[2][:, :], 0.0), tag="zvc1")

        load_weight("wq", "V", "SY")
        load_weight("wk", "S", "G")

        t1 = L.emit("S", [tok["dec"]], lambda h: h.activation(
            out=dsc1[0:1, :], in_=dec_sb[0:1, :], func=ACT.Abs), tag="absd")
        t2 = L.emit("S", [t1], lambda h: h.activation(
            out=dsc1[0:1, :], in_=dsc1[0:1, :], func=ACT.Ln), tag="lnd")
        t3 = L.emit("V", [tok["dec"]], lambda h: h.tensor_scalar(
            out=dsc2[0:1, :], in0=dec_sb[0:1, :], scalar1=1e30, scalar2=1.0,
            op0=ALU.mult, op1=ALU.min), tag="sgn1")
        t4 = L.emit("V", [t3], lambda h: h.tensor_scalar(
            out=dsc2[0:1, :], in0=dsc2[0:1, :], scalar1=-1.0, scalar2=None,
            op0=ALU.max), tag="sgn2")
        t5 = L.emit("T", [tok["pos"], t2], lambda h: h.matmul(
            ps_d[:, 0:16], pos_sb[0:1, :], dsc1[0:1, :], start=True, stop=True),
            tag="dfmm")
        t6 = L.emit("T", [tok["onesf"], t4, t5], lambda h: h.matmul(
            ps_d[:, 16:32], onesf[0:1, :], dsc2[0:1, :], start=True, stop=True),
            tag="sgbmm")
        clnscale = sbuf("clnscale", [128, 1])
        cconst = sbuf("cconst", [128, 5])
        tcl = L.emit("V", [], lambda h: h.memset(clnscale[:, :],
                                                 float(np.log(SCALE))), tag="clnsc")
        _cc = [(0, EPS), (1, -0.5), (2, MAGIC), (3, 1.5), (4, -1.0)]
        tconsts = []
        for _i, _v in _cc:
            tconsts.append(L.emit("V", [], lambda h, _i=_i, _v=_v: h.memset(
                cconst[:, _i:_i + 1], _v), tag=f"cc{_i}"))
        tok["cconst"] = tconsts[-1]
        t7 = L.emit("S", [t5, tcl], lambda h: h.activation(
            out=dfac[:, :], in_=ps_d[:, 0:16], func=ACT.Exp,
            bias=clnscale[:, 0:1]), tag="dfexp")
        t8 = L.emit("V", [t6, tok["par"]], lambda h: h.tensor_scalar(
            out=dtmp[:, :], in0=ps_d[:, 16:32], scalar1=-1.0,
            scalar2=par_sb[:, 0:1], op0=ALU.add, op1=ALU.mult), tag="dfc1")
        t9 = L.emit("V", [t8], lambda h: h.tensor_scalar(
            out=dtmp[:, :], in0=dtmp[:, :], scalar1=1.0, scalar2=None,
            op0=ALU.add), tag="dfc2")
        t10 = L.emit("V", [t9, t7], lambda h: h.tensor_mul(
            out=dfac[:, :], in0=dfac[:, :], in1=dtmp[:, :]), tag="dfc3")
        t11 = L.emit("V", [t10], lambda h: h.memset(dfac[0:1, :], SCALE), tag="dfr0")
        t12 = L.emit("V", [t11], lambda h: h.tensor_copy(
            out=dfbf[:, :], in_=dfac[:, :]), tag="dfbf")
        m2toks = []
        prev_t = t12
        for hh in range(NH):
            a = L.emit("T", [prev_t, tok["i128"], t6, t7, t8], lambda h, hh=hh: h.matmul(
                ps_d[0:1, 128:256],
                dfbf[:, hh:hh + 1], i128[:, :],
                start=True, stop=True), tag=f"m2t{hh}")
            b = L.emit("S", [a], lambda h, hh=hh: h.activation(
                out=m2row[0:1, :], in_=ps_d[0:1, 128:256],
                func=ACT.Copy), tag=f"m2r{hh}")
            c = L.emit("T", [b, tok["ones"]], lambda h, hh=hh: h.matmul(
                ps_d[0:64, 640:768],
                ones_bf[0:1, 0:64], m2row[0:1, :], start=True, stop=True),
                tag=f"m2b{hh}")
            d_ = L.emit("V", [c], lambda h, hh=hh: h.tensor_copy(
                out=m2[(hh % 2) * 64:(hh % 2) * 64 + 64, (hh // 2) * 128:(hh // 2) * 128 + 128],
                in_=ps_d[0:64, 640:768]),
                tag=f"m2e{hh}")
            m2toks.append(d_)
            prev_t = d_
        tok["m2"] = m2toks[-1]
        tok["init_psd_readers"] = [t7, t8] + m2toks

        def rstd_chain(k, t, stats_dep):
            sm = smt[k][t % 2]
            si = smi[k][t % 2]
            a = L.emit("V", [stats_dep], lambda h: h.tensor_scalar(
                out=sm[:, 4:5], in0=sm[:, 1:2], scalar1=EPS, scalar2=None,
                op0=ALU.add), tag=f"rs{k}a_{t}")
            b = L.emit("G", [a], lambda h: h.tensor_copy(
                out=sm[:, 5:6], in_=sm[:, 4:5].bitcast(I32)), tag=f"rs{k}b_{t}")
            c = L.emit("V", [b], lambda h: h.tensor_scalar(
                out=sm[:, 6:7], in0=sm[:, 5:6], scalar1=-0.5, scalar2=MAGIC,
                op0=ALU.mult, op1=ALU.add), tag=f"rs{k}c_{t}")
            d_ = L.emit("G", [c], lambda h: h.tensor_copy(
                out=si[:, 0:1], in_=sm[:, 6:7]), tag=f"rs{k}d_{t}")
            e = L.emit("V", [d_], lambda h: h.tensor_copy(
                out=sm[:, 2:3], in_=si[:, 0:1].bitcast(F32)), tag=f"rs{k}e_{t}")
            last = e
            for it in range(2):
                def newt(h, sm=sm):
                    h.tensor_mul(out=sm[:, 6:7], in0=sm[:, 2:3], in1=sm[:, 2:3])
                    h.drain()
                    h.scalar_tensor_tensor(
                        out=sm[:, 6:7], in0=sm[:, 6:7], scalar=-0.5, in1=sm[:, 4:5],
                        op0=ALU.mult, op1=ALU.mult)
                    h.drain()
                    return h.scalar_tensor_tensor(
                        out=sm[:, 2:3], in0=sm[:, 6:7], scalar=1.5, in1=sm[:, 2:3],
                        op0=ALU.add, op1=ALU.mult)
                last = L.emit("V", [last], newt, tag=f"rs{k}n{it}_{t}")
            nb = L.emit("V", [last], lambda h: h.scalar_tensor_tensor(
                out=sm[:, 3:4], in0=sm[:, 0:1], scalar=-1.0, in1=sm[:, 2:3],
                op0=ALU.mult, op1=ALU.mult), tag=f"rs{k}nb_{t}")
            return nb

        prev_aggr = [None, None]
        ln_ctr = [0]

        def ln_stats(k, t, src_ap, deps):
            sm = smt[k][t % 2]
            o = (ln_ctr[0] % 2) * 12
            ln_ctr[0] += 1
            a = L.emit("V", deps, lambda h: (
                h.bn_stats(out=bn12[:, o:o + 6], in_=src_ap[:, 0:512]),
                h.bn_stats(out=bn12[:, o + 6:o + 12], in_=src_ap[:, 512:1024]))[-1],
                tag=f"bns{k}a_{t}", war=[prev_aggr[o // 12]])
            c = L.emit("V", [a], lambda h: h.bn_aggr(
                out=sm[:, 0:2], in_=bn12[:, o:o + 12]), tag=f"bna{k}_{t}")
            prev_aggr[o // 12] = c
            return c

        def transpose_group(t, src_bf, ps, deps, tag):
            def fn(h):
                last = None
                for ci in range(NC):
                    last = h.matmul(ps[:, ci * 128:(ci + 1) * 128],
                                    src_bf[:, ci * 128:(ci + 1) * 128], i128[:, :],
                                    start=True, stop=True)
                return last
            return L.emit("T", deps + [tok["i128"]], fn, tag=tag)

        NTILE = ntiles

        def emit_ln1(t):
            if t >= NTILE:
                return
            s2 = t % 2
            xt = xs[t % 4]
            if ("xin", t) not in tok:
                deps = [tok[("xout", t - 4)]] if t - 4 >= 0 else []
                tok[("xin", t)] = L.emit_dma("G", deps, lambda h:
                                             h.dma_start(out=xs[t % 4][:, :],
                                                         in_=x_d[t * 128:(t + 1) * 128, :]),
                                             tag=f"xin{t}", sem=f"x{t % 4}")
            st1 = ln_stats(1, t, xt, [tok[("xin", t)], tok.get(("ln1b", t - 2))])
            rs1 = rstd_chain(1, t, st1)
            sm1 = smt[1][s2]
            tok[("ln1b", t)] = L.emit("S", [rs1, tok[("xin", t)],
                                            tok.get(("ln1T", t - 2))],
                                      lambda h: h.activation(
                out=lnb[s2][:, :], in_=xt[:, :], func=ACT.Identity,
                bias=sm1[:, 3:4], scale=sm1[:, 2:3]), tag=f"ln1b_{t}")

        def emit_ln1T(t):
            if t >= NTILE:
                return
            s2 = t % 2
            if t >= 2:
                psd_guard = list(tok[("psdfree", t - 2)])
            elif t >= 1:
                psd_guard = [tok[("ln1fmV", t - 1)], tok[("ln1fmS", t - 1)]]
            else:
                psd_guard = list(tok["init_psd_readers"])
            tok[("ln1T", t)] = transpose_group(
                t, lnb[s2], ps_d, [tok[("ln1b", t)]] + psd_guard, f"ln1T_{t}")
            lnfm_guard = [tok.get(("vmm", t - 2))]
            tok[("ln1fmV", t)] = L.emit("V", [tok[("ln1T", t)]] + lnfm_guard,
                                        lambda h: h.tensor_copy(
                out=lnfm0[s2][:, 0:512], in_=ps_d[:, 0:512]), tag=f"ln1fmV_{t}")
            tok[("ln1fmS", t)] = L.emit("S", [tok[("ln1T", t)]] + lnfm_guard,
                                        lambda h: h.activation(
                out=lnfm0[s2][:, 512:1024], in_=ps_d[:, 512:1024], func=ACT.Copy),
                tag=f"ln1fmS_{t}")

        def emit_front(t):
            s2 = t % 2
            xt = xs[t % 4]
            lnfm_ok = [tok[("ln1fmV", t)], tok[("ln1fmS", t)]]
            if t >= 2:
                if t - 2 >= FUSED_T0 and NTILE > FUSED_T0:
                    psa_guard = [tok.get(("norm2", t - 2))]
                else:
                    psa_guard = [tok.get(("h2ev", t - 2))]
            elif t == 1:
                psa_guard = [tok[("exp", 0)], tok[("anorm", 0)]]
            else:
                psa_guard = []

            wq_deps = [cast_tok[("wq", c)] for c in range(NC)]
            wk_deps = [cast_tok[("wk", c)] for c in range(NC)]
            wv_deps = [cast_tok[("wv", c)] for c in range(NC)]

            def qk_mm(h, wname, ps):
                last = None
                for n in range(NC):
                    for c in range(NC):
                        last = h.matmul(
                            ps[:, n * 128:(n + 1) * 128],
                            wsb[wname][:, c * D + n * 128: c * D + n * 128 + 128],
                            lnfm0[s2][:, c * 128:(c + 1) * 128],
                            start=(c == 0), stop=(c == NC - 1))
                return last

            tok[("qmm", t)] = L.emit("T", lnfm_ok + wq_deps + psa_guard,
                                     lambda h: qk_mm(h, "wq", ps_a), tag=f"qmm_{t}")
            if t == 1:
                psb_guard = [tok[("exp", 0)], tok[("anorm", 0)]]
            elif t - 3 >= FUSED_T0 and NTILE > FUSED_T0:
                psb_guard = [tok.get(("tanh", t - 3))]
            else:
                psb_guard = [tok.get(("tanh", t - 2))]
            tok[("kmm", t)] = L.emit("T", lnfm_ok + wk_deps + psb_guard,
                                     lambda h: qk_mm(h, "wk", ps_b), tag=f"kmm_{t}")

            def v_mm(h):
                h.matmul(ps_c[:, 0:512], ones_bf[0:1, :], bvr[0:1, 0:512],
                         start=True, stop=False)
                h.matmul(ps_c[:, 512:1024], ones_bf[0:1, :], bvr[0:1, 512:1024],
                         start=True, stop=False)
                last = None
                for c in range(NC):
                    for half in range(2):
                        last = h.matmul(
                            ps_c[:, half * 512:(half + 1) * 512],
                            lnfm0[s2][:, c * 128:(c + 1) * 128],
                            wsb["wv"][:, c * D + half * 512: c * D + (half + 1) * 512],
                            start=False, stop=(c == NC - 1))
                return last
            psc_guard = ([tok.get(("aofmV", t - 2)), tok.get(("aofmS", t - 2))]
                         if t >= 2 else ([tok[("xc", 0)], tok[("negmax2", 0)]]
                                         if t == 1 else []))
            tok[("vmm", t)] = L.emit(
                "T", lnfm_ok + wv_deps + [tok["row_bv"], tok["ones"]] + psc_guard,
                v_mm, tag=f"vmm_{t}")

            qp_guard = [tok.get(("scores", t - 1))]
            def q_evac(h):
                last = None
                for ci in range(NC):
                    last = h.scalar_tensor_tensor(
                        out=qp[s2][:, ci * 128:(ci + 1) * 128],
                        in0=ps_a[:, ci * 128:(ci + 1) * 128],
                        scalar=bqf[:, ci:ci + 1], in1=m2[:, ci * 128:(ci + 1) * 128],
                        op0=ALU.add, op1=ALU.mult)
                return last
            tok[("qev", t)] = L.emit("V", [tok[("qmm", t)], tok["bq"], tok["m2"]]
                                     + qp_guard, q_evac, tag=f"qev_{t}")

            kz_guard = ([tok[("scores", t - 1)]] if t else [tok["zkz0"], tok["zkz1"]])
            def kz_evac_v(h):
                last = None
                for hh in range(0, 8):
                    r0 = (hh % 2) * 64
                    last = h.tensor_scalar(
                        out=kz[s2][r0:r0 + 64, hh * 128:(hh + 1) * 128],
                        in0=ps_b[r0:r0 + 64, (hh // 2) * 128:(hh // 2 + 1) * 128],
                        scalar1=bkf[r0:r0 + 64, hh // 2:hh // 2 + 1], scalar2=None,
                        op0=ALU.add)
                return last
            def kz_evac_s(h):
                last = None
                for hh in range(8, NH):
                    r0 = (hh % 2) * 64
                    last = h.activation(
                        out=kz[s2][r0:r0 + 64, hh * 128:(hh + 1) * 128],
                        in_=ps_b[r0:r0 + 64, (hh // 2) * 128:(hh // 2 + 1) * 128],
                        func=ACT.Identity, bias=bkf[r0:r0 + 64, hh // 2:hh // 2 + 1])
                return last
            tok[("kzV", t)] = L.emit("V", [tok[("kmm", t)], tok["bk"]] + kz_guard,
                                     kz_evac_v, tag=f"kzV_{t}")
            tok[("kzS", t)] = L.emit("S", [tok[("kmm", t)], tok["bk"]] + kz_guard,
                                     kz_evac_s, tag=f"kzS_{t}")

            vz_guard = ([tok.get(("outhi", t - 2)), tok.get(("vcdma", t - 2))]
                        if t >= 2 else [tok["zvz0"], tok["zvz1"]])
            def vz_evac_v(h):
                last = None
                for hh in range(0, 8):
                    c0 = hh * 128 + (hh % 2) * 64
                    last = h.tensor_copy(out=vz[s2][:, c0:c0 + 64],
                                         in_=ps_c[:, hh * 64:(hh + 1) * 64])
                return last
            def vz_evac_s(h):
                last = None
                for hh in range(8, NH):
                    c0 = hh * 128 + (hh % 2) * 64
                    last = h.activation(out=vz[s2][:, c0:c0 + 64],
                                        in_=ps_c[:, hh * 64:(hh + 1) * 64],
                                        func=ACT.Identity)
                return last
            tok[("vzV", t)] = L.emit("V", [tok[("vmm", t)]] + vz_guard,
                                     vz_evac_v, tag=f"vzV_{t}")
            tok[("vzS", t)] = L.emit("S", [tok[("vmm", t)]] + vz_guard,
                                     vz_evac_s, tag=f"vzS_{t}")

            kz_prev = kz[1 - s2]
            def scores_mm(h):
                last = None
                for ci in range(NC):
                    for r in range(2):
                        hh = 2 * ci + r
                        ps = ps_a if hh < 8 else ps_b
                        col = (hh % 8) * 128
                        last = h.matmul(ps[:, col:col + 128],
                                        qp[s2][:, ci * 128:(ci + 1) * 128],
                                        kz[s2][:, hh * 128:(hh + 1) * 128],
                                        start=True, stop=True)
                    for r in range(2):
                        hh = 2 * ci + r
                        cc = (hh % 2) * 8 + hh // 2
                        last = h.matmul(ps_c[:, cc:cc + 1],
                                        qp[s2][:, ci * 128:(ci + 1) * 128],
                                        kz_prev[:, hh * 128 + 127:hh * 128 + 128],
                                        start=True, stop=True)
                return last
            sc_deps = [tok[("qev", t)], tok[("kzV", t)], tok[("kzS", t)],
                       tok[("vzV", t)], tok[("vzS", t)]]
            if t:
                sc_deps += [tok[("kzV", t - 1)], tok[("kzS", t - 1)]]
            else:
                sc_deps += [tok["zkz1"]]
            tok[("scores", t)] = L.emit("T", sc_deps, scores_mm, tag=f"sc_{t}")

            def negmax_fn(h):
                h.reduce_max(out=negmx[:, 0:8].rearrange("p (g o) -> p g o", g=8),
                             in_=ps_a[:, :].rearrange("p (g c) -> p g c", g=8),
                             axis=AX, negate=True)
                return h.reduce_max(out=negmx[:, 8:16].rearrange("p (g o) -> p g o", g=8),
                                    in_=ps_b[:, :].rearrange("p (g c) -> p g c", g=8),
                                    axis=AX, negate=True)
            tok[("negmax", t)] = L.emit("V", [tok[("scores", t)]], negmax_fn,
                                        tag=f"negmax_{t}")
            car_h = ps_c[:, 0:16].rearrange("p (r ci) -> p ci r", r=2)
            negmx3 = negmx[:, :].rearrange("p (ci r) -> p ci r", r=2)
            tok[("negmax2", t)] = L.emit("V", [tok[("negmax", t)]], lambda h:
                                         h.scalar_tensor_tensor(
                out=negmx3, in0=car_h, scalar=-1.0, in1=negmx3,
                op0=ALU.mult, op1=ALU.min), tag=f"negmax2_{t}")

            esbt = esb[s2]
            esb_guard = [tok.get(("athi", t - 2)), tok.get(("atc", t - 2))]
            def exp_fn(h):
                last = None
                for hh in range(NH):
                    ps = ps_a if hh < 8 else ps_b
                    col = (hh % 8) * 128
                    last = h.activation(
                        out=esbt[:, hh * 128:(hh + 1) * 128], in_=ps[:, col:col + 128],
                        func=ACT.Exp, bias=negmx[:, hh:hh + 1],
                        accum_out=zsum[:, hh:hh + 1])
                return last
            tok[("exp", t)] = L.emit("S", [tok[("negmax2", t)], tok[("scores", t)]]
                                     + esb_guard, exp_fn, tag=f"exp_{t}")
            xcar3 = xcar[:, :].rearrange("p (ci r) -> p ci r", r=2)
            tok[("xc", t)] = L.emit("V", [tok[("negmax2", t)]], lambda h:
                                    h.tensor_add(out=xcar3, in0=car_h,
                                                 in1=negmx3), tag=f"xc_{t}")
            tok[("ec", t)] = L.emit("S", [tok[("xc", t)], tok[("exp", t)]],
                                    lambda h: h.activation(
                out=ecar[:, :], in_=xcar[:, :], func=ACT.Exp), tag=f"ec_{t}")
            tok[("z", t)] = L.emit("V", [tok[("ec", t)], tok[("exp", t)]],
                                   lambda h: h.tensor_add(
                out=zsum[:, :], in0=zsum[:, :], in1=ecar[:, :]), tag=f"z_{t}")
            tok[("rz", t)] = L.emit("V", [tok[("z", t)]], lambda h: h.reciprocal(
                out=rz[:, :], in_=zsum[:, :]), tag=f"rz_{t}")
            def anorm_fn(h):
                for hh in range(NH):
                    h.tensor_scalar(out=esbt[:, hh * 128:(hh + 1) * 128],
                                    in0=esbt[:, hh * 128:(hh + 1) * 128],
                                    scalar1=rz[:, hh:hh + 1], scalar2=None,
                                    op0=ALU.mult)
                return h.tensor_mul(
                    out=esbt[:, 2048:2064].rearrange("p (r ci) -> p ci r", r=2),
                    in0=ecar[:, :].rearrange("p (ci r) -> p ci r", r=2),
                    in1=rz[:, :].rearrange("p (ci r) -> p ci r", r=2))
            tok[("anorm", t)] = L.emit("V", [tok[("rz", t)], tok[("exp", t)],
                                             tok[("ec", t)]], anorm_fn,
                                       tag=f"anorm_{t}")

            if t < NTILE - 1:
                vs3 = t % 3
                vc_deps = [tok[("vzV", t)], tok[("vzS", t)],
                           tok.get(("aofmV", t - 2)) if t >= 2 else tok["zvc1"]]
                vrow = vz[s2][127:128, :].rearrange("o (ci rc) -> o ci rc", rc=256)
                v1 = L.emit_dma("S", vc_deps, lambda h: h.dma_start(
                    out=vcp[vs3][0:1, :], in_=vrow[:, :, 0:128]),
                    tag=f"vcA_{t}", sem=f"vc{vs3}")
                tok[("vcdma", t)] = L.emit_dma("S", [v1], lambda h:
                                               h.dma_start(out=vcp[vs3][1:2, :],
                                                           in_=vrow[:, :, 128:256]),
                                               tag=f"vcB_{t}", sem=f"vc{vs3}")
            else:
                tok[("vcdma", t)] = None

        def emit_backA(t):
            s2 = t % 2
            xt = xs[t % 4]
            esbt = esb[s2]

            psd_guard = ([tok[("ln1fmV", t + 1)], tok[("ln1fmS", t + 1)]]
                         if t + 1 < NTILE else
                         (list(tok[("psdfree", t - 1)]) if t else []))
            def at_half(h, h0):
                last = None
                for hh in range(h0, h0 + 8):
                    col = (hh % 8) * 128
                    last = h.matmul(ps_d[:, col:col + 128],
                                    esbt[:, hh * 128:(hh + 1) * 128], i128[:, :],
                                    start=True, stop=True)
                return last
            tok[("atlo", t)] = L.emit("T", [tok[("anorm", t)], tok["i128"]]
                                      + psd_guard,
                                      lambda h: at_half(h, 0), tag=f"atlo_{t}")
            at_guard = [tok.get(("outhi", t - 1))]
            tok[("atevloV", t)] = L.emit("V", [tok[("atlo", t)]] + at_guard,
                                         lambda h: h.tensor_copy(
                out=atsb[:, 0:512], in_=ps_d[:, 0:512]), tag=f"atevloV_{t}")
            tok[("atevloS", t)] = L.emit("S", [tok[("atlo", t)]] + at_guard,
                                         lambda h: h.activation(
                out=atsb[:, 512:1024], in_=ps_d[:, 512:1024], func=ACT.Copy),
                tag=f"atevloS_{t}")

            atc_guard = [tok.get(("xc", t + 1))]
            tok[("atc", t)] = L.emit("T", [tok[("anorm", t)], tok["i128"]]
                                     + atc_guard,
                                     lambda h: h.matmul(
                ps_c[0:16, 16:144], esbt[:, 2048:2064], i128[:, :],
                start=True, stop=True), tag=f"atc_{t}")
            tok[("atc16", t)] = L.emit("V", [tok[("atc", t)],
                                             tok.get(("atcp", t - 1))],
                                       lambda h: h.tensor_copy(
                out=atc16[:, :], in_=ps_c[0:16, 16:144]), tag=f"atc16_{t}")
            d1 = L.emit_dma("S", [tok[("atc16", t)], tok.get(("aofmV", t - 1))],
                            lambda h: h.dma_start(out=atcp[0:1, :],
                                                  in_=atc16[0:8, :]),
                            tag=f"atcpA_{t}", sem="atcp")
            tok[("atcp", t)] = L.emit_dma("S", [d1], lambda h: h.dma_start(
                out=atcp[1:2, :], in_=atc16[8:16, :]), tag=f"atcpB_{t}",
                sem="atcp")

            vcp_prev = vcp[(t - 1) % 3]
            def out_half(h, p0):
                last = None
                for ci in range(p0, p0 + 4):
                    for r in range(2):
                        hh = 2 * ci + r
                        last = h.matmul(ps_c[:, ci * 128:(ci + 1) * 128],
                                        vz[s2][:, hh * 128:(hh + 1) * 128],
                                        atsb[:, (hh % 8) * 128:(hh % 8) * 128 + 128],
                                        start=(ci == p0 and r == 0), stop=False)
                return last
            def out_carries(h):
                last = None
                for ci in range(NC):
                    last = h.matmul(ps_c[:, ci * 128:(ci + 1) * 128],
                                    vcp_prev[:, ci * 128:(ci + 1) * 128],
                                    atcp[:, ci * 128:(ci + 1) * 128],
                                    start=False, stop=(ci == 3 or ci == 7))
                return last
            out_deps0 = [tok[("atevloV", t)], tok[("atevloS", t)],
                         tok[("vzV", t)], tok[("vzS", t)],
                         tok[("atc16", t)], tok.get(("xc", t + 1))]
            tok[("outlo", t)] = L.emit("T", out_deps0,
                                       lambda h: out_half(h, 0), tag=f"outlo_{t}")
            tok[("athi", t)] = L.emit("T", [tok[("atevloV", t)],
                                            tok[("atevloS", t)]],
                                      lambda h: at_half(h, 8), tag=f"athi_{t}")
            tok[("atevhiV", t)] = L.emit("V", [tok[("athi", t)],
                                              tok[("outlo", t)]], lambda h:
                                         h.tensor_copy(
                out=atsb[:, 0:512], in_=ps_d[:, 0:512]), tag=f"atevhiV_{t}")
            tok[("atevhiS", t)] = L.emit("S", [tok[("athi", t)],
                                              tok[("outlo", t)]], lambda h:
                                         h.activation(
                out=atsb[:, 512:1024], in_=ps_d[:, 512:1024], func=ACT.Copy),
                tag=f"atevhiS_{t}")
            tok[("psdfree", t)] = [tok[("atevhiV", t)], tok[("atevhiS", t)]]
            tok[("outhi0", t)] = L.emit("T", [tok[("atevhiV", t)],
                                              tok[("atevhiS", t)],
                                              tok[("outlo", t)]],
                                        lambda h: out_half(h, 4), tag=f"outhi0_{t}")
            tok[("outhi", t)] = L.emit("T", [tok[("outhi0", t)], tok[("atcp", t)],
                                             tok[("atc16", t)],
                                             tok[("vcdma", t - 1)] if t else tok["zvc1"]],
                                       out_carries, tag=f"outc_{t}")

            tok[("aofmV", t)] = L.emit("V", [tok[("outhi", t)]], lambda h:
                                       h.tensor_copy(out=aofm[:, 0:512],
                                                     in_=ps_c[:, 0:512]),
                                       tag=f"aofmV_{t}")
            tok[("aofmS", t)] = L.emit("S", [tok[("outhi", t)]], lambda h:
                                       h.activation(out=aofm[:, 512:1024],
                                                    in_=ps_c[:, 512:1024],
                                                    func=ACT.Copy),
                                       tag=f"aofmS_{t}")

            wo_deps = [cast_tok[("wo", c)] for c in range(NC)] + \
                [tok[("aofmV", t)], tok[("aofmS", t)], tok["row_bo"],
                 tok.get(("exp", t + 1)), tok.get(("qev", t + 1))]
            def wo_fn(h):
                h.matmul(ps_a[:, 0:512], ones_bf[0:1, :], bor[0:1, 0:512],
                         start=True, stop=False)
                h.matmul(ps_a[:, 512:1024], ones_bf[0:1, :], bor[0:1, 512:1024],
                         start=True, stop=False)
                last = None
                for c in range(NC):
                    for half in range(2):
                        last = h.matmul(
                            ps_a[:, half * 512:(half + 1) * 512],
                            aofm[:, c * 128:(c + 1) * 128],
                            wsb["wo"][:, c * D + half * 512: c * D + (half + 1) * 512],
                            start=False, stop=(c == NC - 1))
                return last
            tok[("womm", t)] = L.emit("T", wo_deps, wo_fn, tag=f"wo_{t}")

            st2 = ln_stats(2, t, ps_a, [tok[("womm", t)],
                                        tok.get(("norm2", t - 2))])
            rs2 = rstd_chain(2, t, st2)
            sm2 = smt[2][s2]
            nrm_guard = [tok.get(("add4", t - 1))]
            tok[("norm2", t)] = L.emit("S", [rs2, tok[("womm", t)]] + nrm_guard,
                                       lambda h: h.activation(
                out=nrm[0][:, :], in_=ps_a[:, :], func=ACT.Identity,
                bias=sm2[:, 3:4], scale=sm2[:, 2:3]), tag=f"norm2_{t}")
            tok[("add2", t)] = L.emit("G", [tok[("norm2", t)]], lambda h:
                                      h.tensor_add(out=xt[:, :], in0=xt[:, :],
                                                   in1=nrm[0][:, :]), tag=f"add2_{t}")

            st3 = ln_stats(3, t, xt, [tok[("add2", t)],
                                      tok.get(("ln3b", t - 2))])
            rs3 = rstd_chain(3, t, st3)
            sm3 = smt[3][s2]
            tok[("ln3b", t)] = L.emit("S", [rs3, tok.get(("ln3T", t - 1))],
                                      lambda h: h.activation(
                out=lnb3[:, :], in_=xt[:, :], func=ACT.Identity,
                bias=sm3[:, 3:4], scale=sm3[:, 2:3]), tag=f"ln3b_{t}")
            tok[("st2", t)] = st2

        def emit_backB(t):
            s2 = t % 2
            xt = xs[t % 4]
            sm3 = smt[3][s2]
            st2 = tok[("st2", t)]
            ln3fm_guard = [tok.get(("h1mm", t - 1))
                           if (t - 1 < FUSED_T0 or NTILE <= FUSED_T0)
                           else tok.get(("h3mm", t - 1))]
            exp_gd = (t + 2) if (t >= FUSED_T0 and NTILE > FUSED_T0) else (t + 1)
            tok[("ln3T", t)] = transpose_group(
                t, lnb3, ps_b, [tok[("ln3b", t)], tok.get(("exp", exp_gd))],
                f"ln3T_{t}")
            ln3fm_guard = [tok.get(("h1mm", t - 1))
                           if (t - 1 < FUSED_T0 or NTILE <= FUSED_T0)
                           else tok.get(("h3mm", t - 1))]
            tok[("ln3fmV", t)] = L.emit("V", [tok[("ln3T", t)]] + ln3fm_guard,
                                        lambda h: h.tensor_copy(
                out=lnfm1[:, 0:512], in_=ps_b[:, 0:512]), tag=f"ln3fmV_{t}")
            tok[("ln3fmS", t)] = L.emit("S", [tok[("ln3T", t)]] + ln3fm_guard,
                                        lambda h: h.activation(
                out=lnfm1[:, 512:1024], in_=ps_b[:, 512:1024], func=ACT.Copy),
                tag=f"ln3fmS_{t}")
            ln3fm_ok = [tok[("ln3fmV", t)], tok[("ln3fmS", t)]]

            w1_deps = [cast_tok[("w1", c)] for c in range(NC)]
            wa1_deps = [cast_tok[("wa1", c)] for c in range(NC)]
            wa2_deps = [cast_tok[("wa2", c)] for c in range(NC)]

            def ffn_fm_mm(h, wname, src, ps):
                last = None
                for n in range(NC):
                    for c in range(NC):
                        last = h.matmul(
                            ps[:, n * 128:(n + 1) * 128],
                            wsb[wname][:, c * D + n * 128: c * D + n * 128 + 128],
                            src[:, c * 128:(c + 1) * 128],
                            start=(c == 0), stop=(c == NC - 1))
                return last

            def h3_fn_w(h, wname, src):
                h.matmul(ps_b[:, 0:512], ones_bf[0:1, :], bfr[0:1, 0:512],
                         start=True, stop=False)
                h.matmul(ps_b[:, 512:1024], ones_bf[0:1, :], bfr[0:1, 512:1024],
                         start=True, stop=False)
                last = None
                for c in range(NC):
                    for half in range(2):
                        last = h.matmul(
                            ps_b[:, half * 512:(half + 1) * 512],
                            src[:, c * 128:(c + 1) * 128],
                            wsb[wname][:, c * D + half * 512: c * D + (half + 1) * 512],
                            start=False, stop=(c == NC - 1))
                return last

            if t < FUSED_T0 or NTILE <= FUSED_T0:
                gst = [slot_last[2], slot_last[3]] if t == 0 else []
                tok[("h1mm", t)] = L.emit(
                    "T", ln3fm_ok + w1_deps + gst + [tok[("atevhiV", t)],
                                                     tok[("atevhiS", t)]],
                    lambda h: ffn_fm_mm(h, "w1", lnfm1, ps_d), tag=f"h1_{t}")
                tok[("h1ev", t)] = L.emit("V", [tok[("h1mm", t)]], lambda h: (
                    h.tensor_copy(out=h1fm[:, 0:512], in_=ps_d[:, 0:512]),
                    h.tensor_copy(out=h1fm[:, 512:1024], in_=ps_d[:, 512:1024]))[-1],
                    tag=f"h1ev_{t}")
                tok[("psdfree", t)] = [tok[("h1ev", t)]]
                tok[("h2mm", t)] = L.emit(
                    "T", [tok[("h1ev", t)], tok[("norm2", t)], st2]
                    + wa1_deps + gst,
                    lambda h: ffn_fm_mm(h, "wa1", h1fm, ps_a), tag=f"h2_{t}")
                tok[("h2ev", t)] = L.emit("S", [tok[("h2mm", t)]], lambda h: (
                    h.activation(out=h2fm[:, 0:512], in_=ps_a[:, 0:512],
                                 func=ACT.Copy),
                    h.activation(out=h2fm[:, 512:1024], in_=ps_a[:, 512:1024],
                                 func=ACT.Copy))[-1], tag=f"h2ev_{t}")
                h3_deps = [tok[("h2ev", t)], tok["row_bf"]] + wa2_deps + ln3fm_ok
                tok[("h3mm", t)] = L.emit("T", h3_deps,
                                          lambda h: h3_fn_w(h, "wa2", h2fm),
                                          tag=f"h3_{t}")
            else:
                h3_deps = [tok["wa_ready"], tok["row_bf"], tok[("norm2", t)],
                           st2] + ln3fm_ok
                tok[("h3mm", t)] = L.emit("T", h3_deps,
                                          lambda h: h3_fn_w(h, "wa1", lnfm1),
                                          tag=f"h3f_{t}")
                tok[("h2ev", t)] = tok[("norm2", t)]

            t3_guard = [tok.get(("norm4", t - 1)), tok.get(("st4", t - 1)),
                        tok[("add2", t)]]
            tok[("tanh", t)] = L.emit("S", [tok[("h3mm", t)]] + t3_guard,
                                      lambda h: h.activation(
                out=t3sb[:, :], in_=ps_b[:, :], func=ACT.Tanh), tag=f"tanh_{t}")

            st4 = ln_stats(4, t, t3sb, [tok[("tanh", t)],
                                        tok.get(("norm4", t - 2))])
            tok[("st4", t)] = st4
            rs4 = rstd_chain(4, t, st4)
            sm4 = smt[4][s2]
            nrm4_guard = [tok.get(("add2", t))]
            tok[("norm4", t)] = L.emit("S", [rs4, tok[("tanh", t)]] + nrm4_guard,
                                       lambda h: h.activation(
                out=nrm[1][:, :], in_=t3sb[:, :], func=ACT.Identity,
                bias=sm4[:, 3:4], scale=sm4[:, 2:3]), tag=f"norm4_{t}")
            tok[("add4", t)] = L.emit("G", [tok[("norm4", t)]], lambda h:
                                      h.tensor_add(out=xt[:, :], in0=xt[:, :],
                                                   in1=nrm[1][:, :]), tag=f"add4_{t}")
            tok[("xout", t)] = L.emit("G", [tok[("add4", t)]], lambda h:
                                      h.dma_start(out=out_d[t * 128:(t + 1) * 128, :],
                                                  in_=xt[:, :]),
                                      tag=f"xout_{t}", inc_sem=f"do{t % 4}",
                                      inc_amt=16)

        def emit_wa_product():
            w1b = wsb["w1"]; a1b = wsb["wa1"]; a2b = wsb["wa2"]
            pairs = [(r, c) for r in range(NC) for c in range(r, NC)]
            pa_deps = [tok[("h1mm", 2)], tok[("h1ev", 2)], tok["i128"]]
            last_ev = [None]
            for gi in range(0, len(pairs), 4):
                grp = pairs[gi:gi + 4]
                def pa_t(h, grp=grp):
                    last = None
                    for k, (r, c) in enumerate(grp):
                        last = h.matmul(ps_d[:, k * 256:k * 256 + 128],
                                        w1b[:, r * D + c * 128: r * D + c * 128 + 128],
                                        i128[:, :], start=True, stop=True)
                        if r != c:
                            last = h.matmul(ps_d[:, k * 256 + 128:k * 256 + 256],
                                            w1b[:, c * D + r * 128: c * D + r * 128 + 128],
                                            i128[:, :], start=True, stop=True)
                    return last
                tt = L.emit("T", pa_deps + ([last_ev[0]] if last_ev[0] else []),
                            pa_t, tag=f"paT_{gi}")
                def pa_e(h, grp=grp):
                    last = None
                    for k, (r, c) in enumerate(grp):
                        last = h.tensor_copy(
                            out=w1b[:, c * D + r * 128: c * D + r * 128 + 128],
                            in_=ps_d[:, k * 256:k * 256 + 128])
                        if r != c:
                            last = h.tensor_copy(
                                out=w1b[:, r * D + c * 128: r * D + c * 128 + 128],
                                in_=ps_d[:, k * 256 + 128:k * 256 + 256])
                    return last
                last_ev[0] = L.emit("V", [tt], pa_e, tag=f"paE_{gi}")
            pa_done = last_ev[0]
            pc_deps = [pa_done, tok[("h2mm", 2)], tok[("h2ev", 2)],
                       tok[("exp", 3)], tok[("anorm", 3)],
                       slot_last[0], slot_last[1]]
            pd_loads = []
            prev_dma = [None, None]
            for jc in range(NC):
                def pc_t(h, jc=jc):
                    last = None
                    for half in range(2):
                        for mc in range(NC):
                            last = h.matmul(
                                ps_a[:, half * 512:(half + 1) * 512],
                                a1b[:, mc * D + jc * 128: mc * D + jc * 128 + 128],
                                w1b[:, mc * D + half * 512: mc * D + (half + 1) * 512],
                                start=(mc == 0), stop=(mc == NC - 1))
                    return last
                tt = L.emit("T", pc_deps + [d for d in prev_dma if d],
                            pc_t, tag=f"pcT_{jc}")
                dl = []
                for half in range(2):
                    slot = half
                    ev = L.emit("V", [tt] + ([prev_dma[slot]] if prev_dma[slot] else []),
                                lambda h, slot=slot, half=half: h.tensor_copy(
                        out=wstage[slot][:, :],
                        in_=ps_a[:, half * 512:(half + 1) * 512]),
                        tag=f"pcE_{jc}_{half}")
                    dtokp = L.emit_dma("SY", [ev], lambda h, jc=jc, half=half,
                                       slot=slot: h.dma_start(
                        out=t1t_hbm[:, jc * D + half * 512: jc * D + (half + 1) * 512],
                        in_=wstage[slot][:, :]),
                        tag=f"pcD_{jc}_{half}", sem=f"wst{slot}")
                    dl.append(dtokp)
                    prev_dma[slot] = dtokp
                pd_loads.append(dl)
            pc_done = pd_loads[-1][-1]
            pd_casts = []
            for c in range(NC):
                for half in range(2):
                    slot = half
                    ddep = [pd_loads[c][half], pc_done]
                    ddep.append(prev_dma[half])
                    dtok2 = L.emit_dma("SY", ddep + (pd_casts[-2:-1] or []),
                                       lambda h, c=c, half=half, slot=slot:
                                       h.dma_start(
                        out=wstage[slot][:, :],
                        in_=t1t_hbm[:, c * D + half * 512: c * D + (half + 1) * 512]),
                        tag=f"pdD_{c}_{half}", sem=f"wst{slot}")
                    ctok = L.emit("S", [dtok2], lambda h, c=c, half=half, slot=slot:
                                  h.copy(out=w1b[:, c * D + half * 512:
                                                 c * D + (half + 1) * 512],
                                         in_=wstage[slot][:, :]),
                                  tag=f"pdC_{c}_{half}")
                    pd_casts.append(ctok)
            wa_ev = []
            for ic in range(NC):
                def pe_t(h, ic=ic):
                    last = None
                    for half in range(2):
                        for jc in range(NC):
                            last = h.matmul(
                                ps_a[:, half * 512:(half + 1) * 512],
                                w1b[:, jc * D + ic * 128: jc * D + ic * 128 + 128],
                                a2b[:, jc * D + half * 512: jc * D + (half + 1) * 512],
                                start=(jc == 0), stop=(jc == NC - 1))
                    return last
                tt = L.emit("T", pd_casts + [pc_done] +
                            ([wa_ev[-1]] if wa_ev else []),
                            pe_t, tag=f"peT_{ic}")
                ev = L.emit("V", [tt], lambda h, ic=ic: h.tensor_copy(
                    out=a1b[:, ic * D:(ic + 1) * D], in_=ps_a[:, :]),
                    tag=f"peE_{ic}")
                wa_ev.append(ev)
            tok["wa_ready"] = wa_ev[-1]
            tok[("h2ev", 2)] = wa_ev[-1]
            tok[("psdfree", 2)] = [pa_done]

        emit_ln1(0)
        emit_ln1(1)
        emit_ln1T(0)
        load_weight("wv", "V", "SY")
        t_late = min(1, NTILE - 1)
        for t in range(NTILE):
            emit_front(t)
            if t == t_late:
                load_weight("wo", "S", "SY")
                load_weight("wa1", "V", "G")
            if t >= 2 and t - 2 >= FUSED_T0 and NTILE > FUSED_T0:
                emit_backB(t - 2)
            if t >= 1:
                emit_backA(t - 1)
            if t == t_late:
                load_weight("wa2", "S", "G")
                load_weight("w1", "S", "SY", chunks=(0, 2, 4, 6))
                load_weight("w1", "V", "G", chunks=(1, 3, 5, 7))
            if t >= 1 and t - 1 < FUSED_T0:
                emit_backB(t - 1)
            if t == FUSED_T0 and NTILE > FUSED_T0:
                emit_wa_product()
            emit_ln1T(t + 1)
            emit_ln1(t + 2)
        if NTILE >= 1:
            if NTILE - 2 >= FUSED_T0 and NTILE > FUSED_T0:
                emit_backB(NTILE - 2)
            emit_backA(NTILE - 1)
            emit_backB(NTILE - 1)

        if NTILE > 0:
            lastsem = {}
            for e in Ledger.STREAMS:
                for i, rec in enumerate(L.recs[e]):
                    if rec["inc_sem"] is not None and rec["inc_amt"] == 16:
                        lastsem[rec["inc_sem"]] = (e, i + 1)
            L.emit("SY", list(lastsem.values()), lambda h: h.nop(), tag="final")
        else:
            L.emit("SY", [tok["m2"], tok[("xin", 0)], tok[("xin", 1)]],
                   lambda h: h.nop(), tag="final")

        eng_dec = {"T": block.tensor, "V": block.vector, "S": block.scalar,
                   "G": block.gpsimd, "SY": block.sync}
        for e in Ledger.STREAMS:
            eng_dec[e](L.build_runner(e, sems))

    return nc


_CONSTS = None


def _consts():
    global _CONSTS
    if _CONSTS is None:
        bf = ml_dtypes.bfloat16
        _CONSTS = {
            "i128": np.eye(128, dtype=np.float32).astype(bf),
            "ones": np.ones((1, 128), dtype=np.float32).astype(bf),
            "onesf": np.ones((1, 128), dtype=np.float32),
            "pos": np.arange(128, dtype=np.float32).reshape(1, 128),
            "par": (np.arange(128, dtype=np.float32) % 2).reshape(128, 1),
        }
    return _CONSTS


_NC_CACHE = {}


def kernel(L0_attn_x, L0_attn_Wq, L0_attn_bq, L0_attn_Wk, L0_attn_bk,
           L0_attn_Wv, L0_attn_bv, L0_attn_Wo, L0_attn_bo, L0_attn_decay,
           L0_ffn_weight, L0_ffn_aux_weight1, L0_ffn_aux_weight2, L0_ffn_bias,
           _debug=False):
    key = ("dbg" if _debug else "main")
    if key not in _NC_CACHE:
        _NC_CACHE[key] = build_program(debug=_debug)
    nc = _NC_CACHE[key]

    asnp = lambda a: np.ascontiguousarray(np.asarray(a, dtype=np.float32))
    x = asnp(L0_attn_x)
    shared = {
        "w_wq": asnp(L0_attn_Wq), "w_wk": asnp(L0_attn_Wk),
        "w_wv": asnp(L0_attn_Wv), "w_wo": asnp(L0_attn_Wo),
        "w_w1": asnp(L0_ffn_weight), "w_wa1": asnp(L0_ffn_aux_weight1),
        "w_wa2": asnp(L0_ffn_aux_weight2),
        "bq": asnp(L0_attn_bq), "bk": asnp(L0_attn_bk), "bv": asnp(L0_attn_bv),
        "bo": asnp(L0_attn_bo), "bf": asnp(L0_ffn_bias),
        "dec": asnp(L0_attn_decay).reshape(1, NH),
    }
    shared.update(_consts())
    in_maps = [dict(shared, x=x[i]) for i in range(8)]
    res = run_bass_kernel_spmd(nc, in_maps, core_ids=list(range(8)))
    out = np.stack([res.results[i]["out"] for i in range(8)], axis=0)
    if _debug:
        return out, res.results
    return out


# version 14
# speedup vs baseline: 1.0457x; 1.0457x over previous
import contextlib
import numpy as np
import ml_dtypes
import concourse.bass as bass
import concourse.mybir as mybir
from concourse.bass_utils import run_bass_kernel_spmd

F32 = mybir.dt.float32
BF16 = mybir.dt.bfloat16
I32 = mybir.dt.int32
ALU = mybir.AluOpType
ACT = mybir.ActivationFunctionType
AX = mybir.AxisListType.X

D = 1024
NT = 16
NH = 16
NC = 8
EPS = 1e-5
SCALE = float(64) ** -0.5
MAGIC = float(0x5F3759DF)
FUSED_T0 = 3


class Ledger:

    STREAMS = ("T", "V", "S", "G", "SY")

    def __init__(self):
        self.recs = {e: [] for e in self.STREAMS}
        self.sem_owner = {}

    def emit(self, eng, deps, fn, tag="", inc_sem=None, inc_amt=None, war=()):
        lst = self.recs[eng]
        deps = [d for d in deps if d is not None]
        war = [d for d in war if d is not None and d[0] != eng]
        for (de, dv) in list(deps) + war:
            assert dv <= len(self.recs[de]), f"{eng}:{tag} dep on future {de}:{dv}"
            r = self.recs[de][dv - 1]
            if de != eng and r["inc_sem"] is None:
                r["inc_sem"] = de
                r["inc_amt"] = 1
        if inc_sem is not None and inc_sem not in self.STREAMS:
            own = self.sem_owner.setdefault(inc_sem, eng)
            assert own == eng, f"sem {inc_sem} used from {own} and {eng}"
        rec = {"deps": deps, "war": war, "fn": fn, "tag": tag,
               "inc_sem": inc_sem, "inc_amt": inc_amt}
        lst.append(rec)
        return (eng, len(lst))

    def emit_dma(self, eng, deps, fn, tag="", sem=None):
        return self.emit(eng, deps, fn, tag=tag, inc_sem=sem or eng, inc_amt=16)

    def build_runner(self, eng, sems):
        items = self.recs[eng]
        cum = {}
        semval = []
        for rec in items:
            s = rec["inc_sem"]
            if s is not None:
                cum[s] = cum.get(s, 0) + rec["inc_amt"]
            semval.append(dict(cum))

        def runner(h):
            last_wait = {}
            drained = 0
            for i, rec in enumerate(items):
                need_drain = False
                for (de, dv) in rec["deps"] + rec.get("war", []):
                    if de == eng:
                        drec = self.recs[de][dv - 1]
                        if drec["inc_amt"] == 16 and drec["inc_sem"] is not None:
                            sname = drec["inc_sem"]
                            v = 0
                            for j in range(dv):
                                r2 = self.recs[de][j]
                                if r2["inc_sem"] == sname:
                                    v += r2["inc_amt"]
                            if v > last_wait.get(sname, -1):
                                h.wait_ge(sems[sname], v)
                                last_wait[sname] = v
                        elif dv > drained:
                            need_drain = True
                        continue
                    drec = self.recs[de][dv - 1]
                    sname = drec["inc_sem"]
                    v = 0
                    for j in range(dv):
                        r2 = self.recs[de][j]
                        if r2["inc_sem"] == sname:
                            v += r2["inc_amt"]
                    if v > last_wait.get(sname, -1):
                        h.wait_ge(sems[sname], v)
                        last_wait[sname] = v
                if need_drain:
                    h.drain()
                    drained = i
                last = rec["fn"](h)
                if rec["inc_sem"] is not None:
                    assert last is not None, f"{eng} '{rec['tag']}' no instr to inc"
                    last.then_inc(sems[rec["inc_sem"]], rec["inc_amt"])
        return runner


def build_program(debug=False, ntiles=NT, stop_stage=99):
    nc = bass.Bass("TRN2", target_bir_lowering=False)

    def din(name, shape, dt=F32):
        return nc.dram_tensor(name, list(shape), dt, kind="ExternalInput").ap()

    x_d = din("x", [2048, D])
    wnames = ("wq", "wk", "wv", "wo", "w1", "wa1", "wa2")
    w_d = {n: din("w_" + n, [D, D]) for n in wnames}
    bq_d = din("bq", [D]); bk_d = din("bk", [D])
    bv_d = din("bv", [D]); bo_d = din("bo", [D]); bf_d = din("bf", [D])
    dec_d = din("dec", [1, NH])
    i128_d = din("i128", [128, 128], BF16)
    ones_d = din("ones", [1, 128], BF16)
    onesf_d = din("onesf", [1, 128])
    pos_d = din("pos", [1, 128])
    par_d = din("par", [128, 1])
    out_d = nc.dram_tensor("out", [2048, D], F32, kind="ExternalOutput").ap()
    t1t_hbm = nc.dram_tensor("t1t_scratch", [128, NC * D], F32, kind="Internal").ap()

    dbg = {}
    if debug:
        for n, shape, dt in [
                ("ln1b", [128, D], BF16), ("lnfm", [128, D], BF16),
                ("qp", [128, D], BF16), ("kz", [128, 2048], BF16),
                ("vz", [128, 2048], BF16), ("esb", [128, 2064], BF16),
                ("negmx", [128, 16], F32), ("zsum", [128, 16], F32),
                ("atsb", [128, 2048], BF16), ("aofm", [128, D], BF16),
                ("nrm2", [128, D], F32), ("x2", [128, D], F32),
                ("t3", [128, D], F32), ("dfac", [128, 16], F32),
                ("m2", [128, D], BF16), ("aofm1", [128, D], BF16),
                ("x2b", [128, D], F32), ("rstd1", [128, 1], F32),
                ("vcp0", [2, 1024], BF16), ("atcp1", [2, 1024], BF16),
                ("h1", [128, D], BF16)]:
            dbg[n] = nc.dram_tensor("dbg_" + n, shape, dt, kind="ExternalOutput").ap()

    st = contextlib.ExitStack()
    with st:
        def sbuf(name, shape, dt=F32):
            return st.enter_context(nc.sbuf_tensor(name, shape, dt))

        def psum(name, shape):
            return st.enter_context(nc.psum_tensor(name, shape, F32))

        wsb = {n: sbuf("sb_" + n, [128, NC * D], BF16) for n in wnames}
        wstage = [sbuf("wst0", [128, 512]), sbuf("wst1", [128, 512])]
        bqf = sbuf("bqf", [128, NC]); bkf = sbuf("bkf", [128, NC])
        brow = sbuf("brow", [1, D])
        bvr = sbuf("bvr", [1, D], BF16); bor = sbuf("bor", [1, D], BF16)
        bfr = sbuf("bfr", [1, D], BF16)
        dec_sb = sbuf("dec_sb", [1, NH]); dsc1 = sbuf("dsc1", [1, NH])
        dsc2 = sbuf("dsc2", [1, NH])
        dfac = sbuf("dfac", [128, NH]); dtmp = sbuf("dtmp", [128, NH])
        dfbf = sbuf("dfbf", [128, NH], BF16)
        m2 = sbuf("m2", [128, D], BF16)
        m2row = sbuf("m2row", [1, 128], BF16)
        i128 = sbuf("i128_sb", [128, 128], BF16)
        ones_bf = sbuf("ones_bf", [1, 128], BF16)
        onesf = sbuf("onesf_sb", [1, 128])
        pos_sb = sbuf("pos_sb", [1, 128])
        par_sb = sbuf("par_sb", [128, 1])

        xs = [sbuf(f"x{i}", [128, D]) for i in range(4)]
        lnb = [sbuf(f"lnb{i}", [128, D], BF16) for i in range(2)]
        lnb3 = sbuf("lnb3", [128, D], BF16)
        lnfm0 = [sbuf(f"lnfm0{i}", [128, D], BF16) for i in range(2)]
        lnfm1 = sbuf("lnfm1", [128, D], BF16)
        qp = [sbuf("qp0", [128, D], BF16)]
        qp.append(qp[0])
        kz = [sbuf(f"kz{i}", [128, 2048], BF16) for i in range(2)]
        vz = [sbuf(f"vz{i}", [128, 2048], BF16) for i in range(2)]
        vcp = [sbuf(f"vcp{i}", [2, 1024], BF16) for i in range(3)]
        esb = [sbuf(f"esb{i}", [128, 2064], BF16) for i in range(2)]
        negmx = sbuf("negmx", [128, NH])
        xcar = sbuf("xcar", [128, NH])
        zsum = sbuf("zsum", [128, NH])
        ecar = sbuf("ecar", [128, NH])
        rz = sbuf("rz", [128, NH])
        atsb = sbuf("atsb", [128, 1024], BF16)
        atc16 = sbuf("atc16", [16, 128], BF16)
        atcp = sbuf("atcp", [2, 1024], BF16)
        aofm = sbuf("aofm", [128, D], BF16)
        nrm = [sbuf("nrm0", [128, D])]
        nrm.append(nrm[0])
        t3sb = nrm[0]
        h1fm = sbuf("h1fm", [128, D], BF16)
        h2fm = sbuf("h2fm", [128, D], BF16)
        gslots = [h1fm[:, :].bitcast(F32), h2fm[:, :].bitcast(F32)]
        bn12 = sbuf("bn12", [128, 24])
        smt = {k: [sbuf(f"sm_{k}{i}", [128, 8]) for i in range(2)]
               for k in (1, 2, 3, 4)}
        smi = {k: [sbuf(f"smi_{k}{i}", [128, 1], I32) for i in range(2)]
               for k in (1, 2, 3, 4)}

        ps_a = psum("ps_a", [128, 1024])
        ps_b = psum("ps_b", [128, 1024])
        ps_c = psum("ps_c", [128, 1024])
        ps_d = psum("ps_d", [128, 1024])

        sems = {e: st.enter_context(nc.semaphore("sem_" + e))
                for e in Ledger.STREAMS}
        for extra in ("cst", "brow", "wst0", "wst1", "wsy", "wg",
                      "x0", "x1", "x2", "x3", "xp0", "xp1",
                      "vc0", "vc1", "vc2", "atcp", "do0", "do1", "do2", "do3"):
            sems[extra] = st.enter_context(nc.semaphore("sem_" + extra))
        block = st.enter_context(nc.Block())

        L = Ledger()
        tok = {}

        sl = {
            "wst0": wstage[0][:, :], "wst1": wstage[1][:, :],
            "esb0a": esb[0][:, 0:1024].bitcast(F32),
            "esb0b": esb[0][:, 1024:2048].bitcast(F32),
            "esb1a": esb[1][:, 0:1024].bitcast(F32),
            "esb1b": esb[1][:, 1024:2048].bitcast(F32),
            "gst0": gslots[0], "gst1": gslots[1],
            "kz1a": kz[1][:, 0:1024].bitcast(F32),
            "kz1b": kz[1][:, 1024:2048].bitcast(F32),
            "vz1a": vz[1][:, 0:1024].bitcast(F32),
            "vz1b": vz[1][:, 1024:2048].bitcast(F32),
            "lnfm1f": lnfm1[:, :].bitcast(F32),
        }
        QSEM = {"SY": "wsy", "G": "wg"}
        slot_state = {k: None for k in sl}
        borrow_last = {}
        cast_tok = {}

        def load_weight(name, eng, q, slots_use):
            pieces = [(c, pi) for c in range(NC) for pi in range(2)]
            B = len(slots_use)
            for b0 in range(0, len(pieces), B):
                grp = pieces[b0:b0 + B]
                last_d = None
                for (c, pi), s in zip(grp, slots_use):
                    dep = [slot_state[s]] if slot_state[s] else []
                    last_d = L.emit_dma(q, dep, lambda h, c=c, pi=pi, s=s:
                                        h.dma_start(
                        out=sl[s],
                        in_=w_d[name][c * 128:(c + 1) * 128,
                                      pi * 512:(pi + 1) * 512]),
                        tag=f"d_{name}{c}_{pi}", sem=QSEM[q])
                for (c, pi), s in zip(grp, slots_use):
                    o0 = c * D + pi * 512
                    if eng == "S":
                        def castfn(h, o0=o0, name=name, s=s):
                            return h.copy(out=wsb[name][:, o0:o0 + 512],
                                          in_=sl[s])
                    else:
                        def castfn(h, o0=o0, name=name, s=s):
                            return h.tensor_copy(
                                out=wsb[name][:, o0:o0 + 512], in_=sl[s])
                    ct = L.emit(eng, [last_d], castfn, tag=f"c_{name}{c}_{pi}")
                    slot_state[s] = ct
                    borrow_last[s] = ct
                    if pi == 1:
                        cast_tok[(name, c)] = ct

        for t in (0, 1):
            tok[("xin", t)] = L.emit_dma(
                "SY", [], lambda h, t=t: h.dma_start(
                    out=xs[t][:, :], in_=x_d[t * 128:(t + 1) * 128, :]),
                tag=f"xin{t}", sem=f"xp{t}")

        L.emit_dma("SY", [], lambda h: h.dma_start(
            out=bqf[:, :], in_=bq_d.rearrange("(c p) -> p c", p=128),
            allow_slow_non_contiguous=True), tag="bq", sem="cst")
        L.emit_dma("SY", [], lambda h: h.dma_start(
            out=bkf[:, :], in_=bk_d.rearrange("(c p) -> p c", p=128),
            allow_slow_non_contiguous=True), tag="bk", sem="cst")
        last_cst = None
        for nm, dst, src in (("dec", dec_sb, dec_d), ("i128", i128, i128_d),
                             ("ones", ones_bf, ones_d), ("onesf", onesf, onesf_d),
                             ("pos", pos_sb, pos_d), ("par", par_sb, par_d)):
            last_cst = L.emit_dma("SY", [], lambda h, dst=dst, src=src:
                                  h.dma_start(out=dst[:, :], in_=src[:, :]),
                                  tag=nm, sem="cst")
        for nm in ("bq", "bk", "dec", "i128", "ones", "onesf", "pos", "par"):
            tok[nm] = last_cst

        prev = None
        for nm, dst, src in (("bv", bvr, bv_d), ("bo", bor, bo_d), ("bf", bfr, bf_d)):
            dtok = L.emit_dma("SY", [prev] if prev else [], lambda h, src=src:
                              h.dma_start(out=brow[0:1, :], in_=src[None, :]),
                              tag="row_" + nm, sem="brow")
            prev = L.emit("S", [dtok], lambda h, dst=dst: h.activation(
                out=dst[0:1, :], in_=brow[0:1, :], func=ACT.Copy), tag="cast_" + nm)
            tok["row_" + nm] = prev

        tok["zkz0"] = L.emit("V", [], lambda h: h.memset(kz[0][:, :], 0.0), tag="zkz0")
        tok["zvz0"] = L.emit("V", [], lambda h: h.memset(vz[0][:, :], 0.0), tag="zvz0")
        tok["zvc1"] = L.emit("V", [], lambda h: h.memset(vcp[2][:, :], 0.0), tag="zvc1")

        load_weight("wq", "V", "SY",
                    ["wst0", "wst1", "esb0a", "esb0b", "esb1a", "esb1b"])
        load_weight("wk", "S", "SY",
                    ["wst0", "wst1", "esb0a", "esb0b", "esb1a", "esb1b"])
        load_weight("w1", "G", "G",
                    ["gst0", "gst1", "kz1a", "kz1b", "vz1a", "vz1b"])

        t1 = L.emit("S", [tok["dec"]], lambda h: h.activation(
            out=dsc1[0:1, :], in_=dec_sb[0:1, :], func=ACT.Abs), tag="absd")
        t2 = L.emit("S", [t1], lambda h: h.activation(
            out=dsc1[0:1, :], in_=dsc1[0:1, :], func=ACT.Ln), tag="lnd")
        t3 = L.emit("V", [tok["dec"]], lambda h: h.tensor_scalar(
            out=dsc2[0:1, :], in0=dec_sb[0:1, :], scalar1=1e30, scalar2=1.0,
            op0=ALU.mult, op1=ALU.min), tag="sgn1")
        t4 = L.emit("V", [t3], lambda h: h.tensor_scalar(
            out=dsc2[0:1, :], in0=dsc2[0:1, :], scalar1=-1.0, scalar2=None,
            op0=ALU.max), tag="sgn2")
        t5 = L.emit("T", [tok["pos"], t2], lambda h: h.matmul(
            ps_d[:, 0:16], pos_sb[0:1, :], dsc1[0:1, :], start=True, stop=True),
            tag="dfmm")
        t6 = L.emit("T", [tok["onesf"], t4, t5], lambda h: h.matmul(
            ps_d[:, 16:32], onesf[0:1, :], dsc2[0:1, :], start=True, stop=True),
            tag="sgbmm")
        clnscale = sbuf("clnscale", [128, 1])
        cconst = sbuf("cconst", [128, 5])
        tcl = L.emit("V", [], lambda h: h.memset(clnscale[:, :],
                                                 float(np.log(SCALE))), tag="clnsc")
        _cc = [(0, EPS), (1, -0.5), (2, MAGIC), (3, 1.5), (4, -1.0)]
        tconsts = []
        for _i, _v in _cc:
            tconsts.append(L.emit("V", [], lambda h, _i=_i, _v=_v: h.memset(
                cconst[:, _i:_i + 1], _v), tag=f"cc{_i}"))
        tok["cconst"] = tconsts[-1]
        t7 = L.emit("S", [t5, tcl], lambda h: h.activation(
            out=dfac[:, :], in_=ps_d[:, 0:16], func=ACT.Exp,
            bias=clnscale[:, 0:1]), tag="dfexp")
        t8 = L.emit("V", [t6, tok["par"]], lambda h: h.tensor_scalar(
            out=dtmp[:, :], in0=ps_d[:, 16:32], scalar1=-1.0,
            scalar2=par_sb[:, 0:1], op0=ALU.add, op1=ALU.mult), tag="dfc1")
        t9 = L.emit("V", [t8], lambda h: h.tensor_scalar(
            out=dtmp[:, :], in0=dtmp[:, :], scalar1=1.0, scalar2=None,
            op0=ALU.add), tag="dfc2")
        t10 = L.emit("V", [t9, t7], lambda h: h.tensor_mul(
            out=dfac[:, :], in0=dfac[:, :], in1=dtmp[:, :]), tag="dfc3")
        t11 = L.emit("V", [t10], lambda h: h.memset(dfac[0:1, :], SCALE), tag="dfr0")
        t12 = L.emit("V", [t11], lambda h: h.tensor_copy(
            out=dfbf[:, :], in_=dfac[:, :]), tag="dfbf")
        m2toks = []
        prev_t = t12
        for hh in range(NH):
            a = L.emit("T", [prev_t, tok["i128"], t6, t7, t8], lambda h, hh=hh: h.matmul(
                ps_d[0:1, 128:256],
                dfbf[:, hh:hh + 1], i128[:, :],
                start=True, stop=True), tag=f"m2t{hh}")
            b = L.emit("S", [a], lambda h, hh=hh: h.activation(
                out=m2row[0:1, :], in_=ps_d[0:1, 128:256],
                func=ACT.Copy), tag=f"m2r{hh}")
            c = L.emit("T", [b, tok["ones"]], lambda h, hh=hh: h.matmul(
                ps_d[0:64, 640:768],
                ones_bf[0:1, 0:64], m2row[0:1, :], start=True, stop=True),
                tag=f"m2b{hh}")
            d_ = L.emit("V", [c], lambda h, hh=hh: h.tensor_copy(
                out=m2[(hh % 2) * 64:(hh % 2) * 64 + 64, (hh // 2) * 128:(hh // 2) * 128 + 128],
                in_=ps_d[0:64, 640:768]),
                tag=f"m2e{hh}")
            m2toks.append(d_)
            prev_t = d_
        tok["m2"] = m2toks[-1]
        tok["init_psd_readers"] = [t7, t8] + m2toks

        def rstd_chain(k, t, stats_dep):
            sm = smt[k][t % 2]
            si = smi[k][t % 2]
            a = L.emit("V", [stats_dep], lambda h: h.tensor_scalar(
                out=sm[:, 4:5], in0=sm[:, 1:2], scalar1=EPS, scalar2=None,
                op0=ALU.add), tag=f"rs{k}a_{t}")
            b = L.emit("S", [a], lambda h: h.copy(
                out=sm[:, 5:6], in_=sm[:, 4:5].bitcast(I32)), tag=f"rs{k}b_{t}")
            c = L.emit("V", [b], lambda h: h.tensor_scalar(
                out=sm[:, 6:7], in0=sm[:, 5:6], scalar1=-0.5, scalar2=MAGIC,
                op0=ALU.mult, op1=ALU.add), tag=f"rs{k}c_{t}")
            d_ = L.emit("S", [c], lambda h: h.copy(
                out=si[:, 0:1], in_=sm[:, 6:7]), tag=f"rs{k}d_{t}")
            e = L.emit("V", [d_], lambda h: h.tensor_copy(
                out=sm[:, 2:3], in_=si[:, 0:1].bitcast(F32)), tag=f"rs{k}e_{t}")
            last = e
            for it in range(2):
                def newt(h, sm=sm):
                    h.tensor_mul(out=sm[:, 6:7], in0=sm[:, 2:3], in1=sm[:, 2:3])
                    h.drain()
                    h.scalar_tensor_tensor(
                        out=sm[:, 6:7], in0=sm[:, 6:7], scalar=-0.5, in1=sm[:, 4:5],
                        op0=ALU.mult, op1=ALU.mult)
                    h.drain()
                    return h.scalar_tensor_tensor(
                        out=sm[:, 2:3], in0=sm[:, 6:7], scalar=1.5, in1=sm[:, 2:3],
                        op0=ALU.add, op1=ALU.mult)
                last = L.emit("V", [last], newt, tag=f"rs{k}n{it}_{t}")
            nb = L.emit("V", [last], lambda h: h.scalar_tensor_tensor(
                out=sm[:, 3:4], in0=sm[:, 0:1], scalar=-1.0, in1=sm[:, 2:3],
                op0=ALU.mult, op1=ALU.mult), tag=f"rs{k}nb_{t}")
            return nb

        prev_aggr = [None, None]
        ln_ctr = [0]

        def ln_stats(k, t, src_ap, deps):
            sm = smt[k][t % 2]
            o = (ln_ctr[0] % 2) * 12
            ln_ctr[0] += 1
            a = L.emit("V", deps, lambda h: (
                h.bn_stats(out=bn12[:, o:o + 6], in_=src_ap[:, 0:512]),
                h.bn_stats(out=bn12[:, o + 6:o + 12], in_=src_ap[:, 512:1024]))[-1],
                tag=f"bns{k}a_{t}", war=[prev_aggr[o // 12]])
            c = L.emit("V", [a], lambda h: h.bn_aggr(
                out=sm[:, 0:2], in_=bn12[:, o:o + 12]), tag=f"bna{k}_{t}")
            prev_aggr[o // 12] = c
            return c

        def transpose_group(t, src_bf, ps, deps, tag):
            def fn(h):
                last = None
                for ci in range(NC):
                    last = h.matmul(ps[:, ci * 128:(ci + 1) * 128],
                                    src_bf[:, ci * 128:(ci + 1) * 128], i128[:, :],
                                    start=True, stop=True)
                return last
            return L.emit("T", deps + [tok["i128"]], fn, tag=tag)

        NTILE = ntiles

        def emit_ln1(t):
            if t >= NTILE:
                return
            s2 = t % 2
            xt = xs[t % 4]
            if ("xin", t) not in tok:
                deps = [tok[("xout", t - 4)]] if t - 4 >= 0 else []
                tok[("xin", t)] = L.emit_dma("G", deps, lambda h:
                                             h.dma_start(out=xs[t % 4][:, :],
                                                         in_=x_d[t * 128:(t + 1) * 128, :]),
                                             tag=f"xin{t}", sem=f"x{t % 4}")
            st1 = ln_stats(1, t, xt, [tok[("xin", t)], tok.get(("ln1b", t - 2))])
            rs1 = rstd_chain(1, t, st1)
            sm1 = smt[1][s2]
            tok[("ln1b", t)] = L.emit("S", [rs1, tok[("xin", t)],
                                            tok.get(("ln1T", t - 2))],
                                      lambda h: h.activation(
                out=lnb[s2][:, :], in_=xt[:, :], func=ACT.Identity,
                bias=sm1[:, 3:4], scale=sm1[:, 2:3]), tag=f"ln1b_{t}")

        def emit_ln1T(t):
            if t >= NTILE:
                return
            s2 = t % 2
            if t >= 3:
                psd_guard = list(tok[("psdfree", t - 3)])
            elif t >= 1:
                psd_guard = [tok[("ln1fmV", t - 1)], tok[("ln1fmS", t - 1)]]
            else:
                psd_guard = list(tok["init_psd_readers"])
            tok[("ln1T", t)] = transpose_group(
                t, lnb[s2], ps_d, [tok[("ln1b", t)]] + psd_guard, f"ln1T_{t}")
            lnfm_guard = [tok.get(("vmm", t - 2))]
            tok[("ln1fmV", t)] = L.emit("V", [tok[("ln1T", t)]] + lnfm_guard,
                                        lambda h: h.tensor_copy(
                out=lnfm0[s2][:, 0:512], in_=ps_d[:, 0:512]), tag=f"ln1fmV_{t}")
            tok[("ln1fmS", t)] = L.emit("S", [tok[("ln1T", t)]] + lnfm_guard,
                                        lambda h: h.activation(
                out=lnfm0[s2][:, 512:1024], in_=ps_d[:, 512:1024], func=ACT.Copy),
                tag=f"ln1fmS_{t}")

        def emit_front(t):
            s2 = t % 2
            xt = xs[t % 4]
            lnfm_ok = [tok[("ln1fmV", t)], tok[("ln1fmS", t)]]
            if t >= 2:
                if t - 2 >= FUSED_T0 and NTILE > FUSED_T0:
                    psa_guard = [tok.get(("norm2", t - 2))]
                else:
                    psa_guard = [tok.get(("h2ev", t - 2))]
            elif t == 1:
                psa_guard = [tok[("exp", 0)], tok[("anorm", 0)]]
            else:
                psa_guard = []

            wq_deps = [cast_tok[("wq", c)] for c in range(NC)]
            wk_deps = [cast_tok[("wk", c)] for c in range(NC)]
            wv_deps = [cast_tok[("wv", c)] for c in range(NC)]

            def qk_mm(h, wname, ps):
                last = None
                for n in range(NC):
                    for c in range(NC):
                        last = h.matmul(
                            ps[:, n * 128:(n + 1) * 128],
                            wsb[wname][:, c * D + n * 128: c * D + n * 128 + 128],
                            lnfm0[s2][:, c * 128:(c + 1) * 128],
                            start=(c == 0), stop=(c == NC - 1))
                return last

            tok[("qmm", t)] = L.emit("T", lnfm_ok + wq_deps + psa_guard,
                                     lambda h: qk_mm(h, "wq", ps_a), tag=f"qmm_{t}")
            if t == 1:
                psb_guard = [tok[("exp", 0)], tok[("anorm", 0)]]
            elif t - 3 >= FUSED_T0 and NTILE > FUSED_T0:
                psb_guard = [tok.get(("tanh", t - 3))]
            else:
                psb_guard = [tok.get(("tanh", t - 2))]
            tok[("kmm", t)] = L.emit("T", lnfm_ok + wk_deps + psb_guard,
                                     lambda h: qk_mm(h, "wk", ps_b), tag=f"kmm_{t}")

            def v_mm(h):
                h.matmul(ps_c[:, 0:512], ones_bf[0:1, :], bvr[0:1, 0:512],
                         start=True, stop=False)
                h.matmul(ps_c[:, 512:1024], ones_bf[0:1, :], bvr[0:1, 512:1024],
                         start=True, stop=False)
                last = None
                for c in range(NC):
                    for half in range(2):
                        last = h.matmul(
                            ps_c[:, half * 512:(half + 1) * 512],
                            lnfm0[s2][:, c * 128:(c + 1) * 128],
                            wsb["wv"][:, c * D + half * 512: c * D + (half + 1) * 512],
                            start=False, stop=(c == NC - 1))
                return last
            psc_guard = ([tok.get(("aofmV", t - 2)), tok.get(("aofmS", t - 2))]
                         if t >= 2 else ([tok[("xc", 0)], tok[("negmax2", 0)]]
                                         if t == 1 else []))
            tok[("vmm", t)] = L.emit(
                "T", lnfm_ok + wv_deps + [tok["row_bv"], tok["ones"]] + psc_guard,
                v_mm, tag=f"vmm_{t}")

            qp_guard = [tok.get(("scores", t - 1))]
            def q_evac(h):
                last = None
                for ci in range(NC):
                    last = h.scalar_tensor_tensor(
                        out=qp[s2][:, ci * 128:(ci + 1) * 128],
                        in0=ps_a[:, ci * 128:(ci + 1) * 128],
                        scalar=bqf[:, ci:ci + 1], in1=m2[:, ci * 128:(ci + 1) * 128],
                        op0=ALU.add, op1=ALU.mult)
                return last
            tok[("qev", t)] = L.emit("V", [tok[("qmm", t)], tok["bq"], tok["m2"]]
                                     + qp_guard, q_evac, tag=f"qev_{t}")

            kz_guard = ([tok[("scores", t - 1)]] if t else [tok["zkz0"], tok["zkz1"]])
            def kz_evac_v(h):
                last = None
                for hh in range(0, 8):
                    r0 = (hh % 2) * 64
                    last = h.tensor_scalar(
                        out=kz[s2][r0:r0 + 64, hh * 128:(hh + 1) * 128],
                        in0=ps_b[r0:r0 + 64, (hh // 2) * 128:(hh // 2 + 1) * 128],
                        scalar1=bkf[r0:r0 + 64, hh // 2:hh // 2 + 1], scalar2=None,
                        op0=ALU.add)
                return last
            def kz_evac_s(h):
                last = None
                for hh in range(8, NH):
                    r0 = (hh % 2) * 64
                    last = h.activation(
                        out=kz[s2][r0:r0 + 64, hh * 128:(hh + 1) * 128],
                        in_=ps_b[r0:r0 + 64, (hh // 2) * 128:(hh // 2 + 1) * 128],
                        func=ACT.Identity, bias=bkf[r0:r0 + 64, hh // 2:hh // 2 + 1])
                return last
            tok[("kzV", t)] = L.emit("V", [tok[("kmm", t)], tok["bk"]] + kz_guard,
                                     kz_evac_v, tag=f"kzV_{t}")
            tok[("kzS", t)] = L.emit("S", [tok[("kmm", t)], tok["bk"]] + kz_guard,
                                     kz_evac_s, tag=f"kzS_{t}")

            vz_guard = ([tok.get(("outhi", t - 2)), tok.get(("vcdma", t - 2))]
                        if t >= 2 else [tok["zvz0"], tok["zvz1"]])
            def vz_evac_v(h):
                last = None
                for hh in range(0, 8):
                    c0 = hh * 128 + (hh % 2) * 64
                    last = h.tensor_copy(out=vz[s2][:, c0:c0 + 64],
                                         in_=ps_c[:, hh * 64:(hh + 1) * 64])
                return last
            def vz_evac_s(h):
                last = None
                for hh in range(8, NH):
                    c0 = hh * 128 + (hh % 2) * 64
                    last = h.activation(out=vz[s2][:, c0:c0 + 64],
                                        in_=ps_c[:, hh * 64:(hh + 1) * 64],
                                        func=ACT.Identity)
                return last
            tok[("vzV", t)] = L.emit("V", [tok[("vmm", t)]] + vz_guard,
                                     vz_evac_v, tag=f"vzV_{t}")
            tok[("vzS", t)] = L.emit("S", [tok[("vmm", t)]] + vz_guard,
                                     vz_evac_s, tag=f"vzS_{t}")

            kz_prev = kz[1 - s2]
            def scores_mm(h):
                last = None
                for ci in range(NC):
                    for r in range(2):
                        hh = 2 * ci + r
                        ps = ps_a if hh < 8 else ps_b
                        col = (hh % 8) * 128
                        last = h.matmul(ps[:, col:col + 128],
                                        qp[s2][:, ci * 128:(ci + 1) * 128],
                                        kz[s2][:, hh * 128:(hh + 1) * 128],
                                        start=True, stop=True)
                    for r in range(2):
                        hh = 2 * ci + r
                        cc = (hh % 2) * 8 + hh // 2
                        last = h.matmul(ps_c[:, cc:cc + 1],
                                        qp[s2][:, ci * 128:(ci + 1) * 128],
                                        kz_prev[:, hh * 128 + 127:hh * 128 + 128],
                                        start=True, stop=True)
                return last
            sc_deps = [tok[("qev", t)], tok[("kzV", t)], tok[("kzS", t)],
                       tok[("vzV", t)], tok[("vzS", t)]]
            if t:
                sc_deps += [tok[("kzV", t - 1)], tok[("kzS", t - 1)]]
            else:
                sc_deps += [tok["zkz1"]]
            tok[("scores", t)] = L.emit("T", sc_deps, scores_mm, tag=f"sc_{t}")

            def negmax_fn(h):
                h.reduce_max(out=negmx[:, 0:8].rearrange("p (g o) -> p g o", g=8),
                             in_=ps_a[:, :].rearrange("p (g c) -> p g c", g=8),
                             axis=AX, negate=True)
                return h.reduce_max(out=negmx[:, 8:16].rearrange("p (g o) -> p g o", g=8),
                                    in_=ps_b[:, :].rearrange("p (g c) -> p g c", g=8),
                                    axis=AX, negate=True)
            tok[("negmax", t)] = L.emit("V", [tok[("scores", t)]], negmax_fn,
                                        tag=f"negmax_{t}")
            car_h = ps_c[:, 0:16].rearrange("p (r ci) -> p ci r", r=2)
            negmx3 = negmx[:, :].rearrange("p (ci r) -> p ci r", r=2)
            tok[("negmax2", t)] = L.emit("V", [tok[("negmax", t)]], lambda h:
                                         h.scalar_tensor_tensor(
                out=negmx3, in0=car_h, scalar=-1.0, in1=negmx3,
                op0=ALU.mult, op1=ALU.min), tag=f"negmax2_{t}")

            esbt = esb[s2]
            esb_guard = [tok.get(("athi", t - 2)), tok.get(("atc", t - 2))]
            if t == 0:
                esb_guard += [borrow_last.get("esb0a"), borrow_last.get("esb0b")]
            elif t == 1:
                esb_guard += [borrow_last.get("esb1a"), borrow_last.get("esb1b")]
            def exp_fn(h):
                last = None
                for hh in range(NH):
                    ps = ps_a if hh < 8 else ps_b
                    col = (hh % 8) * 128
                    last = h.activation(
                        out=esbt[:, hh * 128:(hh + 1) * 128], in_=ps[:, col:col + 128],
                        func=ACT.Exp, bias=negmx[:, hh:hh + 1],
                        accum_out=zsum[:, hh:hh + 1])
                return last
            tok[("exp", t)] = L.emit("S", [tok[("negmax2", t)], tok[("scores", t)]]
                                     + esb_guard, exp_fn, tag=f"exp_{t}")
            xcar3 = xcar[:, :].rearrange("p (ci r) -> p ci r", r=2)
            tok[("xc", t)] = L.emit("V", [tok[("negmax2", t)]], lambda h:
                                    h.tensor_add(out=xcar3, in0=car_h,
                                                 in1=negmx3), tag=f"xc_{t}")
            tok[("ec", t)] = L.emit("S", [tok[("xc", t)], tok[("exp", t)]],
                                    lambda h: h.activation(
                out=ecar[:, :], in_=xcar[:, :], func=ACT.Exp), tag=f"ec_{t}")
            tok[("z", t)] = L.emit("V", [tok[("ec", t)], tok[("exp", t)]],
                                   lambda h: h.tensor_add(
                out=zsum[:, :], in0=zsum[:, :], in1=ecar[:, :]), tag=f"z_{t}")
            tok[("rz", t)] = L.emit("V", [tok[("z", t)]], lambda h: h.reciprocal(
                out=rz[:, :], in_=zsum[:, :]), tag=f"rz_{t}")
            def anorm_fn(h):
                for hh in range(NH):
                    h.tensor_scalar(out=esbt[:, hh * 128:(hh + 1) * 128],
                                    in0=esbt[:, hh * 128:(hh + 1) * 128],
                                    scalar1=rz[:, hh:hh + 1], scalar2=None,
                                    op0=ALU.mult)
                return h.tensor_mul(
                    out=esbt[:, 2048:2064].rearrange("p (r ci) -> p ci r", r=2),
                    in0=ecar[:, :].rearrange("p (ci r) -> p ci r", r=2),
                    in1=rz[:, :].rearrange("p (ci r) -> p ci r", r=2))
            tok[("anorm", t)] = L.emit("V", [tok[("rz", t)], tok[("exp", t)],
                                             tok[("ec", t)]], anorm_fn,
                                       tag=f"anorm_{t}")

            if t < NTILE - 1:
                vs3 = t % 3
                vc_deps = [tok[("vzV", t)], tok[("vzS", t)],
                           tok.get(("aofmV", t - 2)) if t >= 2 else tok["zvc1"]]
                vrow = vz[s2][127:128, :].rearrange("o (ci rc) -> o ci rc", rc=256)
                v1 = L.emit_dma("S", vc_deps, lambda h: h.dma_start(
                    out=vcp[vs3][0:1, :], in_=vrow[:, :, 0:128]),
                    tag=f"vcA_{t}", sem=f"vc{vs3}")
                tok[("vcdma", t)] = L.emit_dma("S", [v1], lambda h:
                                               h.dma_start(out=vcp[vs3][1:2, :],
                                                           in_=vrow[:, :, 128:256]),
                                               tag=f"vcB_{t}", sem=f"vc{vs3}")
            else:
                tok[("vcdma", t)] = None

            emit_ln1T(t + 1)

        def emit_backA(t):
            s2 = t % 2
            xt = xs[t % 4]
            esbt = esb[s2]

            psd_guard = ([tok[("ln1fmV", t + 2)], tok[("ln1fmS", t + 2)]]
                         if t + 2 < NTILE else
                         (list(tok[("psdfree", t - 1)]) if t else []))
            def at_half(h, h0):
                last = None
                for hh in range(h0, h0 + 8):
                    col = (hh % 8) * 128
                    last = h.matmul(ps_d[:, col:col + 128],
                                    esbt[:, hh * 128:(hh + 1) * 128], i128[:, :],
                                    start=True, stop=True)
                return last
            tok[("atlo", t)] = L.emit("T", [tok[("anorm", t)], tok["i128"]]
                                      + psd_guard,
                                      lambda h: at_half(h, 0), tag=f"atlo_{t}")
            at_guard = [tok.get(("outhi", t - 1))]
            tok[("atevloV", t)] = L.emit("V", [tok[("atlo", t)]] + at_guard,
                                         lambda h: h.tensor_copy(
                out=atsb[:, 0:512], in_=ps_d[:, 0:512]), tag=f"atevloV_{t}")
            tok[("atevloS", t)] = L.emit("S", [tok[("atlo", t)]] + at_guard,
                                         lambda h: h.activation(
                out=atsb[:, 512:1024], in_=ps_d[:, 512:1024], func=ACT.Copy),
                tag=f"atevloS_{t}")

            atc_guard = [tok.get(("xc", t + 1))]
            tok[("atc", t)] = L.emit("T", [tok[("anorm", t)], tok["i128"]]
                                     + atc_guard,
                                     lambda h: h.matmul(
                ps_c[0:16, 16:144], esbt[:, 2048:2064], i128[:, :],
                start=True, stop=True), tag=f"atc_{t}")
            tok[("atc16", t)] = L.emit("V", [tok[("atc", t)],
                                             tok.get(("atcp", t - 1))],
                                       lambda h: h.tensor_copy(
                out=atc16[:, :], in_=ps_c[0:16, 16:144]), tag=f"atc16_{t}")
            d1 = L.emit_dma("S", [tok[("atc16", t)], tok.get(("aofmV", t - 1))],
                            lambda h: h.dma_start(out=atcp[0:1, :],
                                                  in_=atc16[0:8, :]),
                            tag=f"atcpA_{t}", sem="atcp")
            tok[("atcp", t)] = L.emit_dma("S", [d1], lambda h: h.dma_start(
                out=atcp[1:2, :], in_=atc16[8:16, :]), tag=f"atcpB_{t}",
                sem="atcp")

            vcp_prev = vcp[(t - 1) % 3]
            def out_half(h, p0):
                last = None
                for ci in range(p0, p0 + 4):
                    for r in range(2):
                        hh = 2 * ci + r
                        last = h.matmul(ps_c[:, ci * 128:(ci + 1) * 128],
                                        vz[s2][:, hh * 128:(hh + 1) * 128],
                                        atsb[:, (hh % 8) * 128:(hh % 8) * 128 + 128],
                                        start=(ci == p0 and r == 0), stop=False)
                return last
            def out_carries(h):
                last = None
                for ci in range(NC):
                    last = h.matmul(ps_c[:, ci * 128:(ci + 1) * 128],
                                    vcp_prev[:, ci * 128:(ci + 1) * 128],
                                    atcp[:, ci * 128:(ci + 1) * 128],
                                    start=False, stop=(ci == 3 or ci == 7))
                return last
            out_deps0 = [tok[("atevloV", t)], tok[("atevloS", t)],
                         tok[("vzV", t)], tok[("vzS", t)],
                         tok[("atc16", t)], tok.get(("xc", t + 1))]
            tok[("outlo", t)] = L.emit("T", out_deps0,
                                       lambda h: out_half(h, 0), tag=f"outlo_{t}")
            tok[("athi", t)] = L.emit("T", [tok[("atevloV", t)],
                                            tok[("atevloS", t)]],
                                      lambda h: at_half(h, 8), tag=f"athi_{t}")
            tok[("atevhiV", t)] = L.emit("V", [tok[("athi", t)],
                                              tok[("outlo", t)]], lambda h:
                                         h.tensor_copy(
                out=atsb[:, 0:512], in_=ps_d[:, 0:512]), tag=f"atevhiV_{t}")
            tok[("atevhiS", t)] = L.emit("S", [tok[("athi", t)],
                                              tok[("outlo", t)]], lambda h:
                                         h.activation(
                out=atsb[:, 512:1024], in_=ps_d[:, 512:1024], func=ACT.Copy),
                tag=f"atevhiS_{t}")
            tok[("psdfree", t)] = [tok[("atevhiV", t)], tok[("atevhiS", t)]]
            tok[("outhi0", t)] = L.emit("T", [tok[("atevhiV", t)],
                                              tok[("atevhiS", t)],
                                              tok[("outlo", t)]],
                                        lambda h: out_half(h, 4), tag=f"outhi0_{t}")
            tok[("outhi", t)] = L.emit("T", [tok[("outhi0", t)], tok[("atcp", t)],
                                             tok[("atc16", t)],
                                             tok[("vcdma", t - 1)] if t else tok["zvc1"]],
                                       out_carries, tag=f"outc_{t}")

            tok[("aofmV", t)] = L.emit("V", [tok[("outhi", t)]], lambda h:
                                       h.tensor_copy(out=aofm[:, 0:512],
                                                     in_=ps_c[:, 0:512]),
                                       tag=f"aofmV_{t}")
            tok[("aofmS", t)] = L.emit("S", [tok[("outhi", t)]], lambda h:
                                       h.activation(out=aofm[:, 512:1024],
                                                    in_=ps_c[:, 512:1024],
                                                    func=ACT.Copy),
                                       tag=f"aofmS_{t}")

            wo_deps = [cast_tok[("wo", c)] for c in range(NC)] + \
                [tok[("aofmV", t)], tok[("aofmS", t)], tok["row_bo"],
                 tok.get(("exp", t + 1)), tok.get(("qev", t + 1))]
            def wo_fn(h):
                h.matmul(ps_a[:, 0:512], ones_bf[0:1, :], bor[0:1, 0:512],
                         start=True, stop=False)
                h.matmul(ps_a[:, 512:1024], ones_bf[0:1, :], bor[0:1, 512:1024],
                         start=True, stop=False)
                last = None
                for c in range(NC):
                    for half in range(2):
                        last = h.matmul(
                            ps_a[:, half * 512:(half + 1) * 512],
                            aofm[:, c * 128:(c + 1) * 128],
                            wsb["wo"][:, c * D + half * 512: c * D + (half + 1) * 512],
                            start=False, stop=(c == NC - 1))
                return last
            tok[("womm", t)] = L.emit("T", wo_deps, wo_fn, tag=f"wo_{t}")

            st2 = ln_stats(2, t, ps_a, [tok[("womm", t)],
                                        tok.get(("norm2", t - 2))])
            rs2 = rstd_chain(2, t, st2)
            sm2 = smt[2][s2]
            nrm_guard = [tok.get(("add4", t - 1))]
            tok[("norm2", t)] = L.emit("S", [rs2, tok[("womm", t)]] + nrm_guard,
                                       lambda h: h.activation(
                out=nrm[0][:, :], in_=ps_a[:, :], func=ACT.Identity,
                bias=sm2[:, 3:4], scale=sm2[:, 2:3]), tag=f"norm2_{t}")
            tok[("add2", t)] = L.emit("G", [tok[("norm2", t)]], lambda h:
                                      h.tensor_add(out=xt[:, :], in0=xt[:, :],
                                                   in1=nrm[0][:, :]), tag=f"add2_{t}")

            st3 = ln_stats(3, t, xt, [tok[("add2", t)],
                                      tok.get(("ln3b", t - 2))])
            rs3 = rstd_chain(3, t, st3)
            sm3 = smt[3][s2]
            tok[("ln3b", t)] = L.emit("S", [rs3, tok.get(("ln3T", t - 1))],
                                      lambda h: h.activation(
                out=lnb3[:, :], in_=xt[:, :], func=ACT.Identity,
                bias=sm3[:, 3:4], scale=sm3[:, 2:3]), tag=f"ln3b_{t}")
            tok[("st2", t)] = st2

        def emit_backB(t):
            s2 = t % 2
            xt = xs[t % 4]
            sm3 = smt[3][s2]
            st2 = tok[("st2", t)]
            ln3fm_guard = [tok.get(("h1mm", t - 1))
                           if (t - 1 < FUSED_T0 or NTILE <= FUSED_T0)
                           else tok.get(("h3mm", t - 1))]
            exp_gd = (t + 2) if (t >= FUSED_T0 and NTILE > FUSED_T0) else (t + 1)
            tok[("ln3T", t)] = transpose_group(
                t, lnb3, ps_b, [tok[("ln3b", t)], tok.get(("exp", exp_gd))],
                f"ln3T_{t}")
            ln3fm_guard = [tok.get(("h1mm", t - 1))
                           if (t - 1 < FUSED_T0 or NTILE <= FUSED_T0)
                           else tok.get(("h3mm", t - 1))]
            if t == 0:
                ln3fm_guard = ln3fm_guard + [borrow_last.get("lnfm1f")]
            tok[("ln3fmV", t)] = L.emit("V", [tok[("ln3T", t)]] + ln3fm_guard,
                                        lambda h: h.tensor_copy(
                out=lnfm1[:, 0:512], in_=ps_b[:, 0:512]), tag=f"ln3fmV_{t}")
            tok[("ln3fmS", t)] = L.emit("S", [tok[("ln3T", t)]] + ln3fm_guard,
                                        lambda h: h.activation(
                out=lnfm1[:, 512:1024], in_=ps_b[:, 512:1024], func=ACT.Copy),
                tag=f"ln3fmS_{t}")
            ln3fm_ok = [tok[("ln3fmV", t)], tok[("ln3fmS", t)]]

            w1_deps = [cast_tok[("w1", c)] for c in range(NC)]
            wa1_deps = [cast_tok[("wa1", c)] for c in range(NC)]
            wa2_deps = [cast_tok[("wa2", c)] for c in range(NC)]

            def ffn_fm_mm(h, wname, src, ps):
                last = None
                for n in range(NC):
                    for c in range(NC):
                        last = h.matmul(
                            ps[:, n * 128:(n + 1) * 128],
                            wsb[wname][:, c * D + n * 128: c * D + n * 128 + 128],
                            src[:, c * 128:(c + 1) * 128],
                            start=(c == 0), stop=(c == NC - 1))
                return last

            def h3_fn_w(h, wname, src):
                h.matmul(ps_b[:, 0:512], ones_bf[0:1, :], bfr[0:1, 0:512],
                         start=True, stop=False)
                h.matmul(ps_b[:, 512:1024], ones_bf[0:1, :], bfr[0:1, 512:1024],
                         start=True, stop=False)
                last = None
                for c in range(NC):
                    for half in range(2):
                        last = h.matmul(
                            ps_b[:, half * 512:(half + 1) * 512],
                            src[:, c * 128:(c + 1) * 128],
                            wsb[wname][:, c * D + half * 512: c * D + (half + 1) * 512],
                            start=False, stop=(c == NC - 1))
                return last

            if t < FUSED_T0 or NTILE <= FUSED_T0:
                gst = ([borrow_last.get("gst0"), borrow_last.get("gst1"),
                        borrow_last.get("lnfm1f")] if t == 0 else [])
                tok[("h1mm", t)] = L.emit(
                    "T", ln3fm_ok + w1_deps + gst + [tok[("atevhiV", t)],
                                                     tok[("atevhiS", t)]],
                    lambda h: ffn_fm_mm(h, "w1", lnfm1, ps_d), tag=f"h1_{t}")
                tok[("h1ev", t)] = L.emit("V", [tok[("h1mm", t)]], lambda h: (
                    h.tensor_copy(out=h1fm[:, 0:512], in_=ps_d[:, 0:512]),
                    h.tensor_copy(out=h1fm[:, 512:1024], in_=ps_d[:, 512:1024]))[-1],
                    tag=f"h1ev_{t}")
                tok[("psdfree", t)] = [tok[("h1ev", t)]]
                tok[("h2mm", t)] = L.emit(
                    "T", [tok[("h1ev", t)], tok[("norm2", t)], st2]
                    + wa1_deps + gst,
                    lambda h: ffn_fm_mm(h, "wa1", h1fm, ps_a), tag=f"h2_{t}")
                tok[("h2ev", t)] = L.emit("S", [tok[("h2mm", t)]], lambda h: (
                    h.activation(out=h2fm[:, 0:512], in_=ps_a[:, 0:512],
                                 func=ACT.Copy),
                    h.activation(out=h2fm[:, 512:1024], in_=ps_a[:, 512:1024],
                                 func=ACT.Copy))[-1], tag=f"h2ev_{t}")
                h3_deps = [tok[("h2ev", t)], tok["row_bf"]] + wa2_deps + ln3fm_ok
                tok[("h3mm", t)] = L.emit("T", h3_deps,
                                          lambda h: h3_fn_w(h, "wa2", h2fm),
                                          tag=f"h3_{t}")
            else:
                h3_deps = [tok["wa_ready"], tok["row_bf"], tok[("norm2", t)],
                           st2] + ln3fm_ok
                tok[("h3mm", t)] = L.emit("T", h3_deps,
                                          lambda h: h3_fn_w(h, "wa1", lnfm1),
                                          tag=f"h3f_{t}")
                tok[("h2ev", t)] = tok[("norm2", t)]

            t3_guard = [tok.get(("norm4", t - 1)), tok.get(("st4", t - 1)),
                        tok[("add2", t)]]
            tok[("tanh", t)] = L.emit("S", [tok[("h3mm", t)]] + t3_guard,
                                      lambda h: h.activation(
                out=t3sb[:, :], in_=ps_b[:, :], func=ACT.Tanh), tag=f"tanh_{t}")

            st4 = ln_stats(4, t, t3sb, [tok[("tanh", t)],
                                        tok.get(("norm4", t - 2))])
            tok[("st4", t)] = st4
            rs4 = rstd_chain(4, t, st4)
            sm4 = smt[4][s2]
            nrm4_guard = [tok.get(("add2", t))]
            tok[("norm4", t)] = L.emit("S", [rs4, tok[("tanh", t)]] + nrm4_guard,
                                       lambda h: h.activation(
                out=nrm[1][:, :], in_=t3sb[:, :], func=ACT.Identity,
                bias=sm4[:, 3:4], scale=sm4[:, 2:3]), tag=f"norm4_{t}")
            tok[("add4", t)] = L.emit("G", [tok[("norm4", t)]], lambda h:
                                      h.tensor_add(out=xt[:, :], in0=xt[:, :],
                                                   in1=nrm[1][:, :]), tag=f"add4_{t}")
            tok[("xout", t)] = L.emit("G", [tok[("add4", t)]], lambda h:
                                      h.dma_start(out=out_d[t * 128:(t + 1) * 128, :],
                                                  in_=xt[:, :]),
                                      tag=f"xout_{t}", inc_sem=f"do{t % 4}",
                                      inc_amt=16)

        def emit_wa_product():
            w1b = wsb["w1"]; a1b = wsb["wa1"]; a2b = wsb["wa2"]
            pairs = [(r, c) for r in range(NC) for c in range(r, NC)]
            pa_deps = [tok[("h1mm", 2)], tok[("h1ev", 2)], tok["i128"]]
            last_ev = [None]
            for gi in range(0, len(pairs), 4):
                grp = pairs[gi:gi + 4]
                def pa_t(h, grp=grp):
                    last = None
                    for k, (r, c) in enumerate(grp):
                        last = h.matmul(ps_d[:, k * 256:k * 256 + 128],
                                        w1b[:, r * D + c * 128: r * D + c * 128 + 128],
                                        i128[:, :], start=True, stop=True)
                        if r != c:
                            last = h.matmul(ps_d[:, k * 256 + 128:k * 256 + 256],
                                            w1b[:, c * D + r * 128: c * D + r * 128 + 128],
                                            i128[:, :], start=True, stop=True)
                    return last
                tt = L.emit("T", pa_deps + ([last_ev[0]] if last_ev[0] else []),
                            pa_t, tag=f"paT_{gi}")
                def pa_e(h, grp=grp):
                    last = None
                    for k, (r, c) in enumerate(grp):
                        last = h.tensor_copy(
                            out=w1b[:, c * D + r * 128: c * D + r * 128 + 128],
                            in_=ps_d[:, k * 256:k * 256 + 128])
                        if r != c:
                            last = h.tensor_copy(
                                out=w1b[:, r * D + c * 128: r * D + c * 128 + 128],
                                in_=ps_d[:, k * 256 + 128:k * 256 + 256])
                    return last
                last_ev[0] = L.emit("V", [tt], pa_e, tag=f"paE_{gi}")
            pa_done = last_ev[0]
            pc_deps = [pa_done, tok[("h2mm", 2)], tok[("h2ev", 2)],
                       tok[("exp", 3)], tok[("anorm", 3)],
                       slot_state["wst0"], slot_state["wst1"]]
            pd_loads = []
            prev_dma = [None, None]
            for jc in range(NC):
                def pc_t(h, jc=jc):
                    last = None
                    for half in range(2):
                        for mc in range(NC):
                            last = h.matmul(
                                ps_a[:, half * 512:(half + 1) * 512],
                                a1b[:, mc * D + jc * 128: mc * D + jc * 128 + 128],
                                w1b[:, mc * D + half * 512: mc * D + (half + 1) * 512],
                                start=(mc == 0), stop=(mc == NC - 1))
                    return last
                tt = L.emit("T", pc_deps + [d for d in prev_dma if d],
                            pc_t, tag=f"pcT_{jc}")
                dl = []
                for half in range(2):
                    slot = half
                    ev = L.emit("V", [tt] + ([prev_dma[slot]] if prev_dma[slot] else []),
                                lambda h, slot=slot, half=half: h.tensor_copy(
                        out=wstage[slot][:, :],
                        in_=ps_a[:, half * 512:(half + 1) * 512]),
                        tag=f"pcE_{jc}_{half}")
                    dtokp = L.emit_dma("SY", [ev], lambda h, jc=jc, half=half,
                                       slot=slot: h.dma_start(
                        out=t1t_hbm[:, jc * D + half * 512: jc * D + (half + 1) * 512],
                        in_=wstage[slot][:, :]),
                        tag=f"pcD_{jc}_{half}", sem=f"wst{slot}")
                    dl.append(dtokp)
                    prev_dma[slot] = dtokp
                pd_loads.append(dl)
            pc_done = pd_loads[-1][-1]
            pd_casts = []
            for c in range(NC):
                for half in range(2):
                    slot = half
                    ddep = [pd_loads[c][half], pc_done]
                    ddep.append(prev_dma[half])
                    dtok2 = L.emit_dma("SY", ddep + (pd_casts[-2:-1] or []),
                                       lambda h, c=c, half=half, slot=slot:
                                       h.dma_start(
                        out=wstage[slot][:, :],
                        in_=t1t_hbm[:, c * D + half * 512: c * D + (half + 1) * 512]),
                        tag=f"pdD_{c}_{half}", sem=f"wst{slot}")
                    ctok = L.emit("S", [dtok2], lambda h, c=c, half=half, slot=slot:
                                  h.copy(out=w1b[:, c * D + half * 512:
                                                 c * D + (half + 1) * 512],
                                         in_=wstage[slot][:, :]),
                                  tag=f"pdC_{c}_{half}")
                    pd_casts.append(ctok)
            wa_ev = []
            for ic in range(NC):
                def pe_t(h, ic=ic):
                    last = None
                    for half in range(2):
                        for jc in range(NC):
                            last = h.matmul(
                                ps_a[:, half * 512:(half + 1) * 512],
                                w1b[:, jc * D + ic * 128: jc * D + ic * 128 + 128],
                                a2b[:, jc * D + half * 512: jc * D + (half + 1) * 512],
                                start=(jc == 0), stop=(jc == NC - 1))
                    return last
                tt = L.emit("T", pd_casts + [pc_done] +
                            ([wa_ev[-1]] if wa_ev else []),
                            pe_t, tag=f"peT_{ic}")
                ev = L.emit("V", [tt], lambda h, ic=ic: h.tensor_copy(
                    out=a1b[:, ic * D:(ic + 1) * D], in_=ps_a[:, :]),
                    tag=f"peE_{ic}")
                wa_ev.append(ev)
            tok["wa_ready"] = wa_ev[-1]
            tok[("h2ev", 2)] = wa_ev[-1]
            tok[("psdfree", 2)] = [pa_done]

        emit_ln1(0)
        emit_ln1(1)
        emit_ln1T(0)
        load_weight("wv", "G", "SY", ["wst0", "wst1", "esb1a", "esb1b"])
        tok["zkz1"] = L.emit("V", [borrow_last["kz1a"], borrow_last["kz1b"]],
                             lambda h: h.memset(kz[1][:, :], 0.0), tag="zkz1")
        tok["zvz1"] = L.emit("V", [borrow_last["vz1a"], borrow_last["vz1b"]],
                             lambda h: h.memset(vz[1][:, :], 0.0), tag="zvz1")
        t_late = min(1, NTILE - 1)
        for t in range(NTILE):
            if t == t_late:
                load_weight("wo", "S", "SY",
                            ["wst0", "wst1", "esb1a", "esb1b"])
            emit_front(t)
            if t == 0:
                for tt in (2, 3):
                    if tt < NTILE:
                        tok[("xin", tt)] = L.emit_dma(
                            "G", [], lambda h, tt=tt: h.dma_start(
                                out=xs[tt % 4][:, :],
                                in_=x_d[tt * 128:(tt + 1) * 128, :]),
                            tag=f"xin{tt}", sem=f"x{tt % 4}")
                load_weight("wa1", "G", "G", ["gst0", "gst1", "lnfm1f"])
            if t >= 2 and t - 2 >= FUSED_T0 and NTILE > FUSED_T0:
                emit_backB(t - 2)
            if t >= 1:
                emit_backA(t - 1)
            if t == t_late:
                load_weight("wa2", "G", "G", ["gst0", "gst1", "lnfm1f"])
            if t >= 1 and t - 1 < FUSED_T0:
                emit_backB(t - 1)
            if t == FUSED_T0 and NTILE > FUSED_T0:
                emit_wa_product()
            emit_ln1(t + 2)
        if NTILE >= 1:
            if NTILE - 2 >= FUSED_T0 and NTILE > FUSED_T0:
                emit_backB(NTILE - 2)
            emit_backA(NTILE - 1)
            emit_backB(NTILE - 1)

        if NTILE > 0:
            lastsem = {}
            for e in Ledger.STREAMS:
                for i, rec in enumerate(L.recs[e]):
                    if rec["inc_sem"] is not None and rec["inc_amt"] == 16:
                        lastsem[rec["inc_sem"]] = (e, i + 1)
            L.emit("SY", list(lastsem.values()), lambda h: h.nop(), tag="final")
        else:
            L.emit("SY", [tok["m2"], tok[("xin", 0)], tok[("xin", 1)]],
                   lambda h: h.nop(), tag="final")

        eng_dec = {"T": block.tensor, "V": block.vector, "S": block.scalar,
                   "G": block.gpsimd, "SY": block.sync}
        for e in Ledger.STREAMS:
            eng_dec[e](L.build_runner(e, sems))

    return nc


_CONSTS = None


def _consts():
    global _CONSTS
    if _CONSTS is None:
        bf = ml_dtypes.bfloat16
        _CONSTS = {
            "i128": np.eye(128, dtype=np.float32).astype(bf),
            "ones": np.ones((1, 128), dtype=np.float32).astype(bf),
            "onesf": np.ones((1, 128), dtype=np.float32),
            "pos": np.arange(128, dtype=np.float32).reshape(1, 128),
            "par": (np.arange(128, dtype=np.float32) % 2).reshape(128, 1),
        }
    return _CONSTS


_NC_CACHE = {}


def kernel(L0_attn_x, L0_attn_Wq, L0_attn_bq, L0_attn_Wk, L0_attn_bk,
           L0_attn_Wv, L0_attn_bv, L0_attn_Wo, L0_attn_bo, L0_attn_decay,
           L0_ffn_weight, L0_ffn_aux_weight1, L0_ffn_aux_weight2, L0_ffn_bias,
           _debug=False):
    key = ("dbg" if _debug else "main")
    if key not in _NC_CACHE:
        _NC_CACHE[key] = build_program(debug=_debug)
    nc = _NC_CACHE[key]

    asnp = lambda a: np.ascontiguousarray(np.asarray(a, dtype=np.float32))
    x = asnp(L0_attn_x)
    shared = {
        "w_wq": asnp(L0_attn_Wq), "w_wk": asnp(L0_attn_Wk),
        "w_wv": asnp(L0_attn_Wv), "w_wo": asnp(L0_attn_Wo),
        "w_w1": asnp(L0_ffn_weight), "w_wa1": asnp(L0_ffn_aux_weight1),
        "w_wa2": asnp(L0_ffn_aux_weight2),
        "bq": asnp(L0_attn_bq), "bk": asnp(L0_attn_bk), "bv": asnp(L0_attn_bv),
        "bo": asnp(L0_attn_bo), "bf": asnp(L0_ffn_bias),
        "dec": asnp(L0_attn_decay).reshape(1, NH),
    }
    shared.update(_consts())
    in_maps = [dict(shared, x=x[i]) for i in range(8)]
    res = run_bass_kernel_spmd(nc, in_maps, core_ids=list(range(8)))
    out = np.stack([res.results[i]["out"] for i in range(8)], axis=0)
    if _debug:
        return out, res.results
    return out
